# version 01
# baseline (speedup 1.0000x reference)
import os
import numpy as np
import ml_dtypes
import concourse.bass as bass
import concourse.mybir as mybir
from concourse.bass_utils import run_bass_kernel_spmd
from contextlib import ExitStack

F32 = mybir.dt.float32
BF16 = mybir.dt.bfloat16
I32 = mybir.dt.int32
U8 = mybir.dt.uint8
AF = mybir.ActivationFunctionType
ALU = mybir.AluOpType
AX = mybir.AxisListType.X

BF = ml_dtypes.bfloat16

NF4 = [-1.0, -0.6961928009986877, -0.5250730514526367, -0.39491748809814453,
       -0.28444138169288635, -0.18477343022823334, -0.09105003625154495, 0.0,
       0.07958029955625534, 0.16093020141124725, 0.24611230194568634,
       0.33791524171829224, 0.44070982933044434, 0.5626170039176941,
       0.7229568362236023, 1.0]

D = 4096
S = 4096
B = 2
TPC = 1024
NT = TPC // 128
NCORE = 8


class Sem:

    def __init__(self, h):
        self.h = h
        self.n = 0

    def inc(self, inst, k=1):
        inst.then_inc(self.h, k)
        self.n += k


def sems(es, nc, names):
    return {n: Sem(es.enter_context(nc.semaphore(n))) for n in names}


def tick(eng, sem, inst):
    sem.inc(inst)
    eng.wait_ge(sem.h, sem.n)


def emit_identity(blk, s_iot, s_id, sem):
    @blk.gpsimd
    def _(g):
        sem.inc(g.iota(s_iot[:], [[1, 128]], channel_multiplier=-1,
                       allow_small_or_imprecise_dtypes=True))

    @blk.vector
    def _(v):
        v.wait_ge(sem.h, sem.n)
        sem.inc(v.tensor_scalar(s_id[:], s_iot[:], 0.0, None, ALU.is_equal))



def emit_ln_proj(nc, x_dram, xn_fm_dram, projs):
    es = ExitStack()
    with es:
        xnbuf = es.enter_context(nc.sbuf_tensor("xnbuf", [128, 32 * 1024], BF16))
        s_x = [es.enter_context(nc.sbuf_tensor(f"s_x{i}", [128, 4096], F32)) for i in range(2)]
        s_xn = es.enter_context(nc.sbuf_tensor("s_xn", [128, 4096], BF16))
        s_junk = es.enter_context(nc.sbuf_tensor("s_junk", [128, 4096], F32))
        s_iot = es.enter_context(nc.sbuf_tensor("s_iot", [128, 128], F32))
        s_id = es.enter_context(nc.sbuf_tensor("s_id", [128, 128], BF16))
        s_ssq = [es.enter_context(nc.sbuf_tensor(f"s_ssq{i}", [128, 1], F32)) for i in range(2)]
        s_sum = [es.enter_context(nc.sbuf_tensor(f"s_sum{i}", [128, 1], F32)) for i in range(2)]
        s_mean = [es.enter_context(nc.sbuf_tensor(f"s_mean{i}", [128, 1], F32)) for i in range(2)]
        s_var = [es.enter_context(nc.sbuf_tensor(f"s_var{i}", [128, 1], F32)) for i in range(2)]
        s_varraw = [es.enter_context(nc.sbuf_tensor(f"s_varraw{i}", [128, 1], F32)) for i in range(2)]
        s_rstd = es.enter_context(nc.sbuf_tensor("s_rstd", [128, 1], F32))
        s_tmp1 = es.enter_context(nc.sbuf_tensor("s_tmp1", [128, 1], F32))
        p_t = [es.enter_context(nc.psum_tensor(f"p_t{i}", [128, 512], BF16)) for i in range(2)]
        sm = sems(es, nc, ["ld0", "ld1", "ident", "sq", "var", "sqrt", "norm", "tp",
                           "tev", "xout", "vt"])
        ld2 = [sm["ld0"], sm["ld1"]]

        with nc.Block() as blk:
            emit_identity(blk, s_iot, s_id, sm["ident"])

            @blk.sync
            def _(sync):
                for tt in range(NT):
                    if tt >= 2:
                        sync.wait_ge(sm["norm"].h, tt - 1)
                    ld2[tt % 2].inc(sync.dma_start(
                        out=s_x[tt % 2][:], in_=x_dram[128 * tt:128 * (tt + 1), :]), 16)

            @blk.scalar
            def _(s):
                for tt in range(NT):
                    s.wait_ge(ld2[tt % 2].h, 16 * (tt // 2 + 1))
                    if tt >= 2:
                        s.wait_ge(sm["var"].h, 2 * (tt - 1))
                    sm["sq"].inc(s.activation(s_junk[:], s_x[tt % 2][:], AF.Square,
                                              accum_out=s_ssq[tt % 2][:]))
                    s.wait_ge(sm["var"].h, 2 * (tt + 1))
                    sm["sqrt"].inc(s.activation(s_var[tt % 2][:], s_varraw[tt % 2][:],
                                                AF.Sqrt))

            @blk.vector
            def _(v):
                vt = sm["vt"]
                for tt in range(NT):
                    v.wait_ge(ld2[tt % 2].h, 16 * (tt // 2 + 1))
                    tick(v, vt, v.reduce_sum(s_sum[tt % 2][:], s_x[tt % 2][:], axis=AX))
                    v.wait_ge(sm["sq"].h, tt + 1)
                    tick(v, vt, v.tensor_scalar(s_mean[tt % 2][:], s_sum[tt % 2][:],
                                                1.0 / D, None, ALU.mult))
                    tick(v, vt, v.tensor_scalar(s_varraw[tt % 2][:], s_ssq[tt % 2][:],
                                                1.0 / D, None, ALU.mult))
                    tick(v, vt, v.tensor_mul(s_tmp1[:], s_mean[tt % 2][:], s_mean[tt % 2][:]))
                    tick(v, vt, v.tensor_sub(s_varraw[tt % 2][:], s_varraw[tt % 2][:], s_tmp1[:]))
                    sm["var"].inc(v.tensor_scalar(s_varraw[tt % 2][:], s_varraw[tt % 2][:],
                                                  1e-5, None, ALU.add), 2)
                    v.wait_ge(sm["sqrt"].h, tt + 1)
                    tick(v, vt, v.reciprocal(s_rstd[:], s_var[tt % 2][:]))
                    tick(v, vt, v.tensor_mul(s_tmp1[:], s_rstd[:], s_rstd[:]))
                    tick(v, vt, v.tensor_mul(s_tmp1[:], s_tmp1[:], s_varraw[tt % 2][:]))
                    tick(v, vt, v.tensor_scalar(s_tmp1[:], s_tmp1[:], -0.5, 1.5,
                                                ALU.mult, ALU.add))
                    tick(v, vt, v.tensor_mul(s_rstd[:], s_rstd[:], s_tmp1[:]))
                    if tt >= 1:
                        v.wait_ge(sm["tp"].h, 8 * tt)
                    sm["norm"].inc(v.tensor_scalar(
                        s_xn[:], s_x[tt % 2][:], s_mean[tt % 2][:], s_rstd[:],
                        ALU.subtract, ALU.mult))
                    for c4 in range(8):
                        g = tt * 8 + c4
                        v.wait_ge(sm["tp"].h, g + 1)
                        dst = xnbuf[:].rearrange("p (c t) -> p c t", t=1024)[
                            :, 4 * c4:4 * c4 + 4, 128 * tt:128 * (tt + 1)]
                        sm["tev"].inc(v.tensor_copy(
                            dst, p_t[c4 % 2][:].rearrange("p (c t) -> p c t", t=128)))

            @blk.tensor
            def _(t):
                t.wait_ge(sm["ident"].h, 2)
                for tt in range(NT):
                    t.wait_ge(sm["norm"].h, tt + 1)
                    for c4 in range(8):
                        g = tt * 8 + c4
                        if g >= 2:
                            t.wait_ge(sm["tev"].h, g - 1)
                        for j in range(4):
                            c = 4 * c4 + j
                            inst = t.transpose(p_t[c4 % 2][:, 128 * j:128 * (j + 1)],
                                               s_xn[:, 128 * c:128 * (c + 1)], s_id[:])
                        sm["tp"].inc(inst)

            @blk.sync
            def _(sync):
                for tt in range(NT):
                    for c4 in range(8):
                        g = tt * 8 + c4
                        sync.wait_ge(sm["tev"].h, g + 1)
                        dst = xn_fm_dram.rearrange("(a p) (b t) -> a p b t", p=128, t=128)[
                            4 * c4:4 * c4 + 4, :, tt, :].rearrange("a p t -> p a t")
                        src = xnbuf[:].rearrange("p (c t) -> p c t", t=1024)[
                            :, 4 * c4:4 * c4 + 4, 128 * tt:128 * (tt + 1)]
                        sm["xout"].inc(sync.dma_start(out=dst, in_=src), 16)
                sync.wait_ge(sm["xout"].h, sm["xout"].n)

        fm_tiles = []
        tm_grps = []
        for pi, pr in enumerate(projs):
            if pr["kind"] == "fm":
                for dt in range(pr["dout"] // 128):
                    fm_tiles.append((pi, dt))
            else:
                for gidx in range((pr["dout"] + 511) // 512):
                    tm_grps.append((pi, gidx))

        wslab = [es.enter_context(nc.sbuf_tensor(f"wslab{i}", [128, 32 * 128], BF16))
                 for i in range(2)]
        vslab = [es.enter_context(nc.sbuf_tensor(f"vslab{i}", [128, 32 * 512], BF16))
                 for i in range(2)]
        s_bias = es.enter_context(nc.sbuf_tensor("s_bias", [128, 32], F32))
        s_vbias = es.enter_context(nc.sbuf_tensor("s_vbias", [128, 1024], F32))
        s_stage = [es.enter_context(nc.sbuf_tensor(f"s_stage{i}", [128, 512], BF16))
                   for i in range(2)]
        p_mm = [es.enter_context(nc.psum_tensor(f"p_mm{i}", [128, 512], F32)) for i in range(2)]
        sm2 = sems(es, nc, ["wld0", "wld1", "vld0", "vld1", "bld", "mm",
                            "ev", "evd0", "evd1"])
        wld2 = [sm2["wld0"], sm2["wld1"]]
        vld2 = [sm2["vld0"], sm2["vld1"]]
        evd2 = [sm2["evd0"], sm2["evd1"]]

        with nc.Block() as blk:
            nfm = 2 * len(fm_tiles)

            @blk.gpsimd
            def _(g):
                for pi, pr in enumerate(projs):
                    if pr["bias"] is not None and pr["kind"] == "fm":
                        ntile = pr["dout"] // 128
                        sm2["bld"].inc(g.dma_start(
                            out=s_bias[:, pr["sbias0"]:pr["sbias0"] + ntile],
                            in_=pr["bias"]), 16)
                    if pr["bias"] is not None and pr["kind"] == "tm":
                        sm2["bld"].inc(g.dma_start(
                            out=s_vbias[:, 0:pr["dout"]], in_=pr["bias"]), 16)
                for i, (pi, dt) in enumerate(fm_tiles):
                    pr = projs[pi]
                    if i >= 2:
                        g.wait_ge(sm2["mm"].h, 2 * (i - 1))
                    src = pr["w"].rearrange("(c p) n -> p c n", p=128)[
                        :, :, pr["col0"] + 128 * dt: pr["col0"] + 128 * (dt + 1)]
                    wld2[i % 2].inc(g.dma_start(out=wslab[i % 2][:].rearrange(
                        "p (c m) -> p c m", m=128), in_=src), 16)
                for i, (pi, gidx) in enumerate(tm_grps):
                    pr = projs[pi]
                    dw = min(512, pr["dout"] - 512 * gidx)
                    if i >= 2:
                        g.wait_ge(sm2["mm"].h, nfm + NT * (i - 1))
                    src = pr["w"].rearrange("(c p) n -> p c n", p=128)[
                        :, :, pr["col0"] + 512 * gidx: pr["col0"] + 512 * gidx + dw]
                    vld2[i % 2].inc(g.dma_start(out=vslab[i % 2][:].rearrange(
                        "p (c m) -> p c m", m=512)[:, :, 0:dw], in_=src), 16)

            @blk.tensor
            def _(t):
                for i, (pi, dt) in enumerate(fm_tiles):
                    t.wait_ge(wld2[i % 2].h, 16 * (i // 2 + 1))
                    for tg in range(2):
                        k = 2 * i + tg
                        if k >= 2:
                            t.wait_ge(sm2["ev"].h, k - 1)
                        for c in range(32):
                            inst = t.matmul(p_mm[k % 2][:],
                                            wslab[i % 2][:, 128 * c:128 * (c + 1)],
                                            xnbuf[:, 1024 * c + 512 * tg:1024 * c + 512 * (tg + 1)],
                                            start=(c == 0), stop=(c == 31))
                        sm2["mm"].inc(inst)
                for i, (pi, gidx) in enumerate(tm_grps):
                    pr = projs[pi]
                    dw = min(512, pr["dout"] - 512 * gidx)
                    t.wait_ge(vld2[i % 2].h, 16 * (i // 2 + 1))
                    for tt in range(NT):
                        k = nfm + i * NT + tt
                        if k >= 2:
                            t.wait_ge(sm2["ev"].h, k - 1)
                        for c in range(32):
                            inst = t.matmul(p_mm[k % 2][:, 0:dw],
                                            xnbuf[:, 1024 * c + 128 * tt:1024 * c + 128 * (tt + 1)],
                                            vslab[i % 2][:, 512 * c:512 * c + dw],
                                            start=(c == 0), stop=(c == 31))
                        sm2["mm"].inc(inst)

            @blk.scalar
            def _(s):
                s.wait_ge(sm2["bld"].h, sm2["bld"].n)
                for i, (pi, dt) in enumerate(fm_tiles):
                    pr = projs[pi]
                    for tg in range(2):
                        k = 2 * i + tg
                        s.wait_ge(sm2["mm"].h, k + 1)
                        if k >= 2:
                            s.wait_ge(evd2[k % 2].h, 16 * (k // 2))
                        if pr["bias"] is not None:
                            bias_ap = s_bias[:, pr["sbias0"] + dt:pr["sbias0"] + dt + 1]
                        else:
                            bias_ap = 0.0
                        sm2["ev"].inc(s.activation(s_stage[k % 2][:], p_mm[k % 2][:],
                                                   AF.Identity, bias=bias_ap))

            @blk.vector
            def _(v):
                for i, (pi, gidx) in enumerate(tm_grps):
                    pr = projs[pi]
                    dw = min(512, pr["dout"] - 512 * gidx)
                    v.wait_ge(sm2["bld"].h, sm2["bld"].n)
                    for tt in range(NT):
                        k = nfm + i * NT + tt
                        v.wait_ge(sm2["mm"].h, k + 1)
                        if k >= 2:
                            v.wait_ge(evd2[k % 2].h, 16 * (k // 2))
                        if pr["bias"] is not None:
                            sm2["ev"].inc(v.tensor_add(
                                s_stage[k % 2][:, 0:dw], p_mm[k % 2][:, 0:dw],
                                s_vbias[:, 512 * gidx:512 * gidx + dw]))
                        else:
                            sm2["ev"].inc(v.tensor_copy(
                                s_stage[k % 2][:, 0:dw], p_mm[k % 2][:, 0:dw]))

            @blk.sync
            def _(sync):
                for i, (pi, dt) in enumerate(fm_tiles):
                    pr = projs[pi]
                    for tg in range(2):
                        k = 2 * i + tg
                        sync.wait_ge(sm2["ev"].h, k + 1)
                        evd2[k % 2].inc(sync.dma_start(
                            out=pr["out"][128 * dt:128 * (dt + 1), 512 * tg:512 * (tg + 1)],
                            in_=s_stage[k % 2][:]), 16)
                nfm = 2 * len(fm_tiles)
                for i, (pi, gidx) in enumerate(tm_grps):
                    pr = projs[pi]
                    dw = min(512, pr["dout"] - 512 * gidx)
                    for tt in range(NT):
                        k = nfm + i * NT + tt
                        sync.wait_ge(sm2["ev"].h, k + 1)
                        evd2[k % 2].inc(sync.dma_start(
                            out=pr["out"][128 * tt:128 * (tt + 1), 512 * gidx:512 * gidx + dw],
                            in_=s_stage[k % 2][:, 0:dw]), 16)
                sync.wait_ge(evd2[0].h, evd2[0].n)
                sync.wait_ge(evd2[1].h, evd2[1].n)



def emit_attn0(nc, q_dram, k_dram, v_dram, wo_dram, x_dram, bias_bcs, y_dram,
               ktile_map=None):
    DH = 1024
    NKB = 8
    scale = 1.0 / 32.0
    es = ExitStack()
    with es:
        qbuf = es.enter_context(nc.sbuf_tensor("qbuf", [128, 8 * 1024], BF16))
        attnfm = es.enter_context(nc.sbuf_tensor("attnfm", [128, 8 * 1024], BF16))
        s_bias = es.enter_context(nc.sbuf_tensor("s_bias", [128, 4096], F32))
        s_bias2 = es.enter_context(nc.sbuf_tensor("s_bias2", [128, 4096], F32))
        kslab = [es.enter_context(nc.sbuf_tensor(f"kslab{i}", [128, 8 * 512], BF16))
                 for i in range(2)]
        vslab = [es.enter_context(nc.sbuf_tensor(f"vslab{i}", [128, 4 * 1024], BF16))
                 for i in range(2)]
        s_row = es.enter_context(nc.sbuf_tensor("s_row", [128, 4096], F32))
        s_probs = es.enter_context(nc.sbuf_tensor("s_probs", [128, 4096], BF16))
        s_probsT = es.enter_context(nc.sbuf_tensor("s_probsT", [128, 4096], BF16))
        s_m = es.enter_context(nc.sbuf_tensor("s_m", [128, 1], F32))
        s_negm = es.enter_context(nc.sbuf_tensor("s_negm", [128, 1], F32))
        s_esum = es.enter_context(nc.sbuf_tensor("s_esum", [128, 1], F32))
        s_rsum = es.enter_context(nc.sbuf_tensor("s_rsum", [128, 1], F32))
        s_iot = es.enter_context(nc.sbuf_tensor("s_iot", [128, 128], F32))
        s_id = es.enter_context(nc.sbuf_tensor("s_id", [128, 128], BF16))
        s_pv = es.enter_context(nc.sbuf_tensor("s_pv", [128, 1024], BF16))
        woslab = [es.enter_context(nc.sbuf_tensor(f"woslab{i}", [128, 8 * 512], BF16))
                  for i in range(2)]
        s_xt = [es.enter_context(nc.sbuf_tensor(f"s_xt{i}", [128, 512], F32))
                for i in range(2)]
        s_yst = [es.enter_context(nc.sbuf_tensor(f"s_yst{i}", [128, 512], F32))
                 for i in range(2)]
        p_s = [es.enter_context(nc.psum_tensor(f"p_s{i}", [128, 512], F32)) for i in range(2)]
        p_T = [es.enter_context(nc.psum_tensor(f"p_T{i}", [128, 512], BF16)) for i in range(2)]
        p_pv = [es.enter_context(nc.psum_tensor(f"p_pv{i}", [128, 512], F32)) for i in range(2)]
        p_o = [es.enter_context(nc.psum_tensor(f"p_o{i}", [128, 512], F32)) for i in range(2)]
        sm = sems(es, nc, ["qld", "bld", "ident", "kld0", "kld1", "vld0", "vld1",
                           "smm", "sev", "max", "exp", "tpp", "tpe", "pvmm", "pvev",
                           "attT", "atev", "wld0", "wld1", "xld0", "xld1", "omm",
                           "oev", "yst0", "yst1", "vt", "st2"])
        kld2 = [sm["kld0"], sm["kld1"]]
        vld2 = [sm["vld0"], sm["vld1"]]
        wld2 = [sm["wld0"], sm["wld1"]]
        xld2 = [sm["xld0"], sm["xld1"]]
        yst2 = [sm["yst0"], sm["yst1"]]

        if ktile_map is None:
            ktile_map = lambda g: 128 * g

        with nc.Block() as blk:
            emit_identity(blk, s_iot, s_id, sm["ident"])

            @blk.sync
            def _(sync):
                src = q_dram.rearrange("(c p) t -> p c t", p=128)
                sm["qld"].inc(sync.dma_start(
                    out=qbuf[:].rearrange("p (c t) -> p c t", t=1024), in_=src), 16)
                for i, bb in enumerate(bias_bcs):
                    sm["bld"].inc(sync.dma_start(
                        out=(s_bias if i == 0 else s_bias2)[:], in_=bb), 16)
                sync.wait_ge(sm["bld"].h, sm["bld"].n)

            @blk.vector
            def _(v):
                v.wait_ge(sm["bld"].h, sm["bld"].n)
                if len(bias_bcs) > 1:
                    v.tensor_add(s_bias[:], s_bias[:], s_bias2[:])
                v.engine_nop().then_inc(sm["bld"].h, 1)
        sm["bld"].n += 1

        with nc.Block() as blk:
            @blk.sync
            def _(sync):
                for qt in range(NT):
                    for kb in range(NKB):
                        i = qt * NKB + kb
                        if i >= 2:
                            sync.wait_ge(sm["smm"].h, i - 1)
                        for g4 in range(4):
                            g = kb * 4 + g4
                            r0 = ktile_map(g)
                            src = k_dram.rearrange("(c p) t -> p c t", p=128)[
                                :, :, r0:r0 + 128]
                            kld2[i % 2].inc(sync.dma_start(
                                out=kslab[i % 2][:].rearrange("p (c t) -> p c t", t=512)[
                                    :, :, 128 * g4:128 * (g4 + 1)],
                                in_=src), 16)

            @blk.gpsimd
            def _(g_):
                for qt in range(NT):
                    for kb in range(NKB):
                        i = qt * NKB + kb
                        if i >= 2:
                            g_.wait_ge(sm["pvmm"].h, i - 1)
                        for g4 in range(4):
                            g = kb * 4 + g4
                            r0 = ktile_map(g)
                            vld2[i % 2].inc(g_.dma_start(
                                out=vslab[i % 2][:, 1024 * g4:1024 * (g4 + 1)],
                                in_=v_dram[r0:r0 + 128, :]), 16)

            @blk.tensor
            def _(t):
                t.wait_ge(sm["ident"].h, 2)
                t.wait_ge(sm["qld"].h, 16)
                for qt in range(NT):
                    for kb in range(NKB):
                        i = qt * NKB + kb
                        t.wait_ge(kld2[i % 2].h, 16 * 4 * (i // 2 + 1))
                        if i >= 2:
                            t.wait_ge(sm["sev"].h, i - 1)
                        for c in range(8):
                            inst = t.matmul(
                                p_s[i % 2][:],
                                qbuf[:, 1024 * c + 128 * qt:1024 * c + 128 * (qt + 1)],
                                kslab[i % 2][:, 512 * c:512 * (c + 1)],
                                start=(c == 0), stop=(c == 7))
                        sm["smm"].inc(inst)
                    t.wait_ge(sm["exp"].h, qt + 1)
                    for c4 in range(8):
                        gi = qt * 8 + c4
                        if gi >= 2:
                            t.wait_ge(sm["tpe"].h, gi - 1)
                        for j in range(4):
                            c = 4 * c4 + j
                            inst = t.transpose(p_T[c4 % 2][:, 128 * j:128 * (j + 1)],
                                               s_probs[:, 128 * c:128 * (c + 1)], s_id[:])
                        sm["tpp"].inc(inst)
                    for kb in range(NKB):
                        i = qt * NKB + kb
                        t.wait_ge(vld2[i % 2].h, 16 * 4 * (i // 2 + 1))
                        t.wait_ge(sm["tpe"].h, qt * 8 + (kb * 4 + 4 + 3) // 4)
                        for dg in range(2):
                            for kc in range(4):
                                c = kb * 4 + kc
                                inst = t.matmul(
                                    p_pv[dg][:],
                                    s_probsT[:, 128 * c:128 * (c + 1)],
                                    vslab[i % 2][:, 1024 * kc + 512 * dg:
                                                  1024 * kc + 512 * (dg + 1)],
                                    start=(kb == 0 and kc == 0),
                                    stop=(kb == NKB - 1 and kc == 3))
                        sm["pvmm"].inc(inst)
                    t.wait_ge(sm["pvev"].h, 2 * (qt + 1))
                    for pair in range(2):
                        gi = qt * 2 + pair
                        if gi >= 2:
                            t.wait_ge(sm["atev"].h, gi - 1)
                        for j in range(4):
                            c = 4 * pair + j
                            inst = t.transpose(p_T[pair % 2][:, 128 * j:128 * (j + 1)],
                                               s_pv[:, 128 * c:128 * (c + 1)], s_id[:])
                        sm["attT"].inc(inst)

            @blk.vector
            def _(v):
                vt = sm["vt"]
                for qt in range(NT):
                    for kb in range(NKB):
                        i = qt * NKB + kb
                        v.wait_ge(sm["smm"].h, i + 1)
                        sm["sev"].inc(v.tensor_scalar(
                            s_row[:, 512 * kb:512 * (kb + 1)], p_s[i % 2][:],
                            scale, None, ALU.mult))
                    tick(v, vt, v.reduce_max(s_m[:], s_row[:], axis=AX))
                    tick(v, vt, v.tensor_scalar(s_negm[:], s_m[:], -1.0, None, ALU.mult))
                    v.engine_nop().then_inc(sm["max"].h, 1)
                    sm["max"].n += 1
                    v.wait_ge(sm["exp"].h, qt + 1)
                    tick(v, vt, v.reciprocal(s_rsum[:], s_esum[:]))
                    for c4 in range(8):
                        gi = qt * 8 + c4
                        v.wait_ge(sm["tpp"].h, gi + 1)
                        dst = s_probsT[:].rearrange("p (c t) -> p c t", t=128)[
                            :, 4 * c4:4 * c4 + 4, :]
                        sm["tpe"].inc(v.tensor_copy(
                            dst, p_T[c4 % 2][:].rearrange("p (c t) -> p c t", t=128)))
                    v.wait_ge(sm["pvmm"].h, (qt + 1) * NKB)
                    for dg in range(2):
                        sm["pvev"].inc(v.tensor_scalar(
                            s_pv[:, 512 * dg:512 * (dg + 1)], p_pv[dg][:],
                            s_rsum[:], None, ALU.mult))
                    for pair in range(2):
                        gi = qt * 2 + pair
                        v.wait_ge(sm["attT"].h, gi + 1)
                        dst = attnfm[:].rearrange("p (c t) -> p c t", t=1024)[
                            :, 4 * pair:4 * pair + 4, 128 * qt:128 * (qt + 1)]
                        sm["atev"].inc(v.tensor_copy(
                            dst, p_T[pair % 2][:].rearrange("p (c t) -> p c t", t=128)))

            @blk.scalar
            def _(s):
                for qt in range(NT):
                    s.wait_ge(sm["max"].h, qt + 1)
                    if qt >= 1:
                        s.wait_ge(sm["tpp"].h, 8 * qt)
                    sm["exp"].inc(s.activation(s_probs[:], s_row[:], AF.Exp,
                                               bias=s_negm[:], accum_out=s_esum[:]))

        with nc.Block() as blk:
            @blk.gpsimd
            def _(g_):
                for og in range(8):
                    if og >= 2:
                        g_.wait_ge(sm["omm"].h, 8 * (og - 1))
                    src = wo_dram.rearrange("(c p) n -> p c n", p=128)[
                        :, :, 512 * og:512 * (og + 1)]
                    wld2[og % 2].inc(g_.dma_start(
                        out=woslab[og % 2][:].rearrange("p (c m) -> p c m", m=512),
                        in_=src), 16)

            @blk.sync
            def _(sync):
                for og in range(8):
                    for qt in range(NT):
                        i = og * NT + qt
                        if i >= 2:
                            sync.wait_ge(sm["oev"].h, i - 1)
                        xld2[i % 2].inc(sync.dma_start(
                            out=s_xt[i % 2][:],
                            in_=x_dram[128 * qt:128 * (qt + 1), 512 * og:512 * (og + 1)]), 16)

            @blk.tensor
            def _(t):
                for og in range(8):
                    t.wait_ge(wld2[og % 2].h, 16 * (og // 2 + 1))
                    for qt in range(NT):
                        i = og * NT + qt
                        if i >= 2:
                            t.wait_ge(sm["oev"].h, i - 1)
                        for c in range(8):
                            inst = t.matmul(
                                p_o[i % 2][:],
                                attnfm[:, 1024 * c + 128 * qt:1024 * c + 128 * (qt + 1)],
                                woslab[og % 2][:, 512 * c:512 * (c + 1)],
                                start=(c == 0), stop=(c == 7))
                        sm["omm"].inc(inst)

            @blk.vector
            def _(v):
                for og in range(8):
                    for qt in range(NT):
                        i = og * NT + qt
                        v.wait_ge(sm["omm"].h, i + 1)
                        v.wait_ge(xld2[i % 2].h, 16 * (i // 2 + 1))
                        if i >= 2:
                            v.wait_ge(yst2[i % 2].h, 16 * (i // 2))
                        v.tensor_add(s_yst[i % 2][:], p_o[i % 2][:],
                                     s_bias[:, 512 * og:512 * (og + 1)])
                        sm["oev"].inc(v.tensor_add(s_yst[i % 2][:], s_yst[i % 2][:],
                                                   s_xt[i % 2][:]))

            @blk.scalar
            def _(s):
                for og in range(8):
                    for qt in range(NT):
                        i = og * NT + qt
                        s.wait_ge(sm["oev"].h, i + 1)
                        yst2[i % 2].inc(s.dma_start(
                            out=y_dram[128 * qt:128 * (qt + 1), 512 * og:512 * (og + 1)],
                            in_=s_yst[i % 2][:]), 16)
                s.wait_ge(yst2[0].h, yst2[0].n)
                s.wait_ge(yst2[1].h, yst2[1].n)


def build_attn0():
    nc = bass.Bass("TRN2", target_bir_lowering=False,
                   detect_race_conditions=(os.environ.get("SIM_NORACE") != "1"))
    q = nc.dram_tensor("q", [1024, TPC], BF16, kind="ExternalInput").ap()
    k = nc.dram_tensor("k", [1024, S], BF16, kind="ExternalInput").ap()
    v = nc.dram_tensor("v", [S, 1024], BF16, kind="ExternalInput").ap()
    wo = nc.dram_tensor("wo", [1024, D], F32, kind="ExternalInput").ap()
    x = nc.dram_tensor("x", [TPC, D], F32, kind="ExternalInput").ap()
    bias1 = nc.dram_tensor("bias1", [128, D], F32, kind="ExternalInput").ap()
    bias2 = nc.dram_tensor("bias2", [128, D], F32, kind="ExternalInput").ap()
    y = nc.dram_tensor("y", [TPC, D], F32, kind="ExternalOutput").ap()
    emit_attn0(nc, q, k, v, wo, x, [bias1, bias2], y)
    return nc



NH = 32
HD = 128


def alibi_slopes(n_heads=NH):
    import math
    closest = 2 ** math.floor(math.log2(n_heads))
    base = 2.0 ** (-2.0 ** (-(math.log2(closest) - 3)))
    slopes = [base ** i for i in range(1, closest + 1)]
    if closest != n_heads:
        extra = 2.0 ** (-2.0 ** (-(math.log2(2 * closest) - 3)))
        n_extra = min(closest, n_heads - closest)
        slopes += [extra ** i for i in range(1, 2 * n_extra + 1, 2)]
    return slopes


def emit_attn1_core(nc, q_dram, k_dram, v_dram, mask_dram, attnfm_dram,
                    ktile_map=None):
    scale = float(HD) ** -0.5
    slopes = alibi_slopes()
    es = ExitStack()
    with es:
        qbuf = es.enter_context(nc.sbuf_tensor("qbuf", [128, 32 * 1024], BF16))
        ahbuf = [es.enter_context(nc.sbuf_tensor(f"ahbuf{i}", [128, 1024], BF16))
                 for i in range(2)]
        k1buf = es.enter_context(nc.sbuf_tensor("k1buf", [128, 4096], BF16))
        v1buf = es.enter_context(nc.sbuf_tensor("v1buf", [128, 32 * 128], BF16))
        s_mask = es.enter_context(nc.sbuf_tensor("s_mask", [128, 1024], F32))
        r0d = es.enter_context(nc.sbuf_tensor("r0d", [128, 8 * 512], F32))
        rhbuf = es.enter_context(nc.sbuf_tensor("rhbuf", [128, 8 * 512], F32))
        probs = es.enter_context(nc.sbuf_tensor("probs", [128, 4096], BF16))
        probsT = es.enter_context(nc.sbuf_tensor("probsT", [128, 4096], BF16))
        s_iot = es.enter_context(nc.sbuf_tensor("s_iot", [128, 128], F32))
        s_id = es.enter_context(nc.sbuf_tensor("s_id", [128, 128], BF16))
        s_srow = es.enter_context(nc.sbuf_tensor("s_srow", [128, 4096], F32))
        sumbuf = es.enter_context(nc.sbuf_tensor("sumbuf", [128, 8], F32))
        s_M = es.enter_context(nc.sbuf_tensor("s_M", [128, 1], F32))
        s_negm = es.enter_context(nc.sbuf_tensor("s_negm", [128, 1], F32))
        s_SS = es.enter_context(nc.sbuf_tensor("s_SS", [128, 1], F32))
        s_rs = es.enter_context(nc.sbuf_tensor("s_rs", [128, 1], F32))
        s_pv = es.enter_context(nc.sbuf_tensor("s_pv", [128, 128], BF16))
        p_s = [es.enter_context(nc.psum_tensor(f"p_s{i}", [128, 512], F32)) for i in range(2)]
        p_T = [es.enter_context(nc.psum_tensor(f"p_T{i}", [128, 512], BF16)) for i in range(2)]
        p_pv = es.enter_context(nc.psum_tensor("p_pv", [128, 128], F32))
        p_pvT = es.enter_context(nc.psum_tensor("p_pvT", [128, 128], BF16))
        sm = sems(es, nc, ["ld", "ident", "r0", "erh", "smm", "mx", "maxd", "exp",
                           "pmul", "tpp", "tpe", "pvmm", "pvev", "pvT", "atev", "vt",
                           "sev", "ao0", "ao1"])
        ao2 = [sm["ao0"], sm["ao1"]]

        if ktile_map is None:
            ktile_map = lambda g: 128 * g

        with nc.Block() as blk:
            emit_identity(blk, s_iot, s_id, sm["ident"])

            @blk.sync
            def _(sync):
                sm["ld"].inc(sync.dma_start(
                    out=qbuf[:].rearrange("p (c t) -> p c t", t=1024),
                    in_=q_dram.rearrange("(c p) t -> p c t", p=128)), 16)
                sm["ld"].inc(sync.dma_start(out=s_mask[:], in_=mask_dram), 16)
                for g in range(32):
                    r0 = ktile_map(g)
                    sm["ld"].inc(sync.dma_start(
                        out=k1buf[:, 128 * g:128 * (g + 1)],
                        in_=k_dram[:, r0:r0 + 128]), 16)
                    sm["ld"].inc(sync.dma_start(
                        out=v1buf[:, 128 * g:128 * (g + 1)],
                        in_=v_dram[r0:r0 + 128, :]), 16)

            @blk.gpsimd
            def _(g):
                for dlt in range(8):
                    sm["r0"].inc(g.iota(r0d[:, 512 * dlt:512 * (dlt + 1)],
                                        [[1, 512]], base=512 * (dlt - 7),
                                        channel_multiplier=-1,
                                        allow_small_or_imprecise_dtypes=True))

            @blk.vector
            def _(v):
                v.wait_ge(sm["r0"].h, 8)
                v.wait_ge(sm["ld"].h, sm["ld"].n)
                for dlt in range(7):
                    v.tensor_add(r0d[:, 512 * dlt:512 * (dlt + 1)],
                                 r0d[:, 512 * dlt:512 * (dlt + 1)],
                                 s_mask[:, 512:1024])
                sm["r0"].inc(v.tensor_add(r0d[:, 512 * 7:512 * 8],
                                          r0d[:, 512 * 7:512 * 8],
                                          s_mask[:, 0:512]))

        with nc.Block() as blk:
            @blk.scalar
            def _(s):
                s.wait_ge(sm["ld"].h, sm["ld"].n)
                s.wait_ge(sm["r0"].h, 9)
                cnt_sev = 0
                cnt_maxd = 0
                cnt_tpp = 0
                for h in range(NH):
                    if h >= 1:
                        s.wait_ge(sm["sev"].h, cnt_sev)
                    for dlt in range(8):
                        sm["erh"].inc(s.activation(
                            rhbuf[:, 512 * dlt:512 * (dlt + 1)],
                            r0d[:, 512 * dlt:512 * (dlt + 1)],
                            AF.Identity, scale=float(slopes[h])))
                    for t in range(NT):
                        cnt_sev += t + 1
                        cnt_maxd += 1
                        s.wait_ge(sm["maxd"].h, cnt_maxd)
                        if cnt_tpp:
                            s.wait_ge(sm["tpp"].h, cnt_tpp)
                        sm["exp"].inc(s.activation(
                            probs[:, 0:512 * (t + 1)], s_srow[:, 0:512 * (t + 1)],
                            AF.Exp, bias=s_negm[:], accum_out=s_SS[:]))
                        cnt_tpp += t + 1

            @blk.tensor
            def _(t_):
                t_.wait_ge(sm["ident"].h, 2)
                t_.wait_ge(sm["ld"].h, sm["ld"].n)
                cnt_tpe = 0
                cnt_exp = 0
                cnt_pvev = 0
                bidx = 0
                gi = 0
                for h in range(NH):
                    for t in range(NT):
                        g0 = h * NT + t
                        for b in range(t + 1):
                            if bidx + b >= 2:
                                t_.wait_ge(sm["sev"].h, bidx + b - 1)
                            t_.matmul(p_s[(bidx + b) % 2][:],
                                      qbuf[:, 1024 * h + 128 * t:1024 * h + 128 * (t + 1)],
                                      k1buf[:, 512 * b:512 * (b + 1)],
                                      start=True, stop=True).then_inc(sm["smm"].h, 1)
                            sm["smm"].n += 1
                        cnt_exp += 1
                        t_.wait_ge(sm["exp"].h, cnt_exp)
                        for b in range(t + 1):
                            if gi >= 2:
                                t_.wait_ge(sm["tpe"].h, gi - 1)
                            for j in range(4):
                                c = 4 * b + j
                                inst = t_.transpose(
                                    p_T[gi % 2][:, 128 * j:128 * (j + 1)],
                                    probs[:, 128 * c:128 * (c + 1)], s_id[:])
                            sm["tpp"].inc(inst)
                            gi += 1
                        for c in range(4 * (t + 1)):
                            t_.wait_ge(sm["tpe"].h, cnt_tpe + c // 4 + 1)
                            t_.matmul(p_pv[:], probsT[:, 128 * c:128 * (c + 1)],
                                      v1buf[:, 128 * c:128 * (c + 1)],
                                      start=(c == 0), stop=(c == 4 * t + 3)
                                      ).then_inc(sm["pvmm"].h, 1)
                            sm["pvmm"].n += 1
                        cnt_tpe += t + 1
                        bidx += t + 1
                        cnt_pvev += 1
                        t_.wait_ge(sm["pvev"].h, cnt_pvev)
                        if g0 >= 1:
                            t_.wait_ge(sm["atev"].h, g0)
                        sm["pvT"].inc(t_.transpose(p_pvT[:], s_pv[:], s_id[:]))

            @blk.gpsimd
            def _(g):
                for h in range(NH):
                    g.wait_ge(sm["atev"].h, 8 * (h + 1))
                    ao2[h % 2].inc(g.dma_start(
                        out=attnfm_dram[128 * h:128 * (h + 1), :],
                        in_=ahbuf[h % 2][:]), 16)

            @blk.vector
            def _(v):
                vt = sm["vt"]
                cnt_mm = 0
                cnt_exp = 0
                cnt_tpp = 0
                cnt_pvmm = 0
                cnt_pvT = 0
                bidx = 0
                for h in range(NH):
                    v.wait_ge(sm["erh"].h, 8 * (h + 1))
                    for t in range(NT):
                        g0 = h * NT + t
                        for b in range(t + 1):
                            v.wait_ge(sm["smm"].h, cnt_mm + b + 1)
                            dlt = b - t + 7
                            sm["sev"].inc(v.scalar_tensor_tensor(
                                s_srow[:, 512 * b:512 * (b + 1)],
                                p_s[(bidx + b) % 2][:], scale,
                                rhbuf[:, 512 * dlt:512 * (dlt + 1)],
                                ALU.mult, ALU.add))
                        v.wait_ge(sm["sev"].h, sm["sev"].n)
                        tick(v, vt, v.reduce_max(s_M[:], s_srow[:, 0:512 * (t + 1)],
                                                 axis=AX))
                        tick(v, vt, v.tensor_scalar(s_negm[:], s_M[:], -1.0, None,
                                                    ALU.mult))
                        v.engine_nop().then_inc(sm["maxd"].h, 1)
                        sm["maxd"].n += 1
                        cnt_exp += 1
                        v.wait_ge(sm["exp"].h, cnt_exp)
                        tick(v, vt, v.reciprocal(s_rs[:], s_SS[:]))
                        for b in range(t + 1):
                            v.wait_ge(sm["tpp"].h, cnt_tpp + b + 1)
                            dst = probsT[:].rearrange("p (c t2) -> p c t2", t2=128)[
                                :, 4 * b:4 * b + 4, :]
                            sm["tpe"].inc(v.tensor_copy(
                                dst, p_T[(cnt_tpp + b) % 2][:].rearrange(
                                    "p (c t2) -> p c t2", t2=128)))
                        cnt_pvmm += 4 * (t + 1)
                        v.wait_ge(sm["pvmm"].h, cnt_pvmm)
                        sm["pvev"].inc(v.tensor_scalar(
                            s_pv[:], p_pv[:], s_rs[:], None, ALU.mult))
                        cnt_pvT += 1
                        v.wait_ge(sm["pvT"].h, cnt_pvT)
                        if t == 0 and h >= 2:
                            v.wait_ge(ao2[h % 2].h, 16 * (h // 2))
                        sm["atev"].inc(v.tensor_copy(
                            ahbuf[h % 2][:, 128 * t:128 * (t + 1)], p_pvT[:]))
                        cnt_tpp += t + 1
                        cnt_mm += t + 1
                        bidx += t + 1

        with nc.Block() as blk:
            @blk.sync
            def _(sync):
                sync.wait_ge(ao2[0].h, ao2[0].n)
                sync.wait_ge(ao2[1].h, ao2[1].n)


def emit_proj_res(nc, act_dram, w_dram, y_in_dram, y_out_dram, nchunk=32):
    es = ExitStack()
    with es:
        actbuf = es.enter_context(nc.sbuf_tensor("actbuf", [128, nchunk * 1024], BF16))
        woslab = [es.enter_context(nc.sbuf_tensor(f"woslab{i}", [128, nchunk * 512], BF16))
                  for i in range(2)]
        s_yt = [es.enter_context(nc.sbuf_tensor(f"s_yt{i}", [128, 512], F32))
                for i in range(2)]
        s_yst = [es.enter_context(nc.sbuf_tensor(f"s_yst{i}", [128, 512], F32))
                 for i in range(2)]
        p_o = [es.enter_context(nc.psum_tensor(f"p_o{i}", [128, 512], F32)) for i in range(2)]
        sm = sems(es, nc, ["ald", "wld0", "wld1", "yld0", "yld1", "omm", "oev",
                           "yst0", "yst1"])
        wld2 = [sm["wld0"], sm["wld1"]]
        yld2 = [sm["yld0"], sm["yld1"]]
        yst2 = [sm["yst0"], sm["yst1"]]

        with nc.Block() as blk:
            @blk.sync
            def _(sync):
                sm["ald"].inc(sync.dma_start(
                    out=actbuf[:].rearrange("p (c t) -> p c t", t=1024),
                    in_=act_dram.rearrange("(c p) t -> p c t", p=128)), 16)

            @blk.gpsimd
            def _(g):
                for og in range(8):
                    if og >= 2:
                        g.wait_ge(sm["omm"].h, 8 * (og - 1))
                    src = w_dram.rearrange("(c p) n -> p c n", p=128)[
                        :, :, 512 * og:512 * (og + 1)]
                    wld2[og % 2].inc(g.dma_start(
                        out=woslab[og % 2][:].rearrange("p (c m) -> p c m", m=512),
                        in_=src), 16)

            @blk.tensor
            def _(t):
                t.wait_ge(sm["ald"].h, 16)
                for og in range(8):
                    t.wait_ge(wld2[og % 2].h, 16 * (og // 2 + 1))
                    for qt in range(NT):
                        i = og * NT + qt
                        if i >= 2:
                            t.wait_ge(sm["oev"].h, i - 1)
                        for c in range(nchunk):
                            inst = t.matmul(
                                p_o[i % 2][:],
                                actbuf[:, 1024 * c + 128 * qt:1024 * c + 128 * (qt + 1)],
                                woslab[og % 2][:, 512 * c:512 * (c + 1)],
                                start=(c == 0), stop=(c == nchunk - 1))
                        sm["omm"].inc(inst)

            @blk.sync
            def _(sync):
                for og in range(8):
                    for qt in range(NT):
                        i = og * NT + qt
                        if i >= 2:
                            sync.wait_ge(sm["oev"].h, i - 1)
                        yld2[i % 2].inc(sync.dma_start(
                            out=s_yt[i % 2][:],
                            in_=y_in_dram[128 * qt:128 * (qt + 1),
                                          512 * og:512 * (og + 1)]), 16)

            @blk.vector
            def _(v):
                for og in range(8):
                    for qt in range(NT):
                        i = og * NT + qt
                        v.wait_ge(sm["omm"].h, i + 1)
                        v.wait_ge(yld2[i % 2].h, 16 * (i // 2 + 1))
                        if i >= 2:
                            v.wait_ge(yst2[i % 2].h, 16 * (i // 2))
                        sm["oev"].inc(v.tensor_add(s_yst[i % 2][:], p_o[i % 2][:],
                                                   s_yt[i % 2][:]))

            @blk.scalar
            def _(s):
                for og in range(8):
                    for qt in range(NT):
                        i = og * NT + qt
                        s.wait_ge(sm["oev"].h, i + 1)
                        yst2[i % 2].inc(s.dma_start(
                            out=y_out_dram[128 * qt:128 * (qt + 1),
                                           512 * og:512 * (og + 1)],
                            in_=s_yst[i % 2][:]), 16)
                s.wait_ge(yst2[0].h, yst2[0].n)
                s.wait_ge(yst2[1].h, yst2[1].n)


def build_attn1_core(with_map=False):
    nc = bass.Bass("TRN2", target_bir_lowering=False,
                   detect_race_conditions=(os.environ.get("SIM_NORACE") != "1"))
    q = nc.dram_tensor("q", [D, TPC], BF16, kind="ExternalInput").ap()
    k = nc.dram_tensor("k", [128, S], BF16, kind="ExternalInput").ap()
    v = nc.dram_tensor("v", [S, 128], BF16, kind="ExternalInput").ap()
    msk = nc.dram_tensor("msk", [128, 1024], F32, kind="ExternalInput").ap()
    attnfm = nc.dram_tensor("attnfm", [D, TPC], BF16, kind="ExternalOutput").ap()
    emit_attn1_core(nc, q, k, v, msk, attnfm)
    return nc


def build_proj_res():
    nc = bass.Bass("TRN2", target_bir_lowering=False,
                   detect_race_conditions=(os.environ.get("SIM_NORACE") != "1"))
    act = nc.dram_tensor("act", [D, TPC], BF16, kind="ExternalInput").ap()
    w = nc.dram_tensor("w", [D, D], F32, kind="ExternalInput").ap()
    y_in = nc.dram_tensor("y_in", [TPC, D], F32, kind="ExternalInput").ap()
    y_out = nc.dram_tensor("y_out", [TPC, D], F32, kind="ExternalOutput").ap()
    emit_proj_res(nc, act, w, y_in, y_out)
    return nc



RANKS = (8, 16, 32, 64)


def emit_ffn0(nc, xn_dram, w_dram, a_dram, b_dram, gw_dram, gb_dram, y_in_dram,
              y_out_dram):
    es = ExitStack()
    with es:
        xnbuf = es.enter_context(nc.sbuf_tensor("xnbuf", [128, 32 * 1024], BF16))
        wTbuf = es.enter_context(nc.sbuf_tensor("wTbuf", [128, 32 * 512], BF16))
        s_w = [es.enter_context(nc.sbuf_tensor(f"s_w{i}", [128, 4096], BF16))
               for i in range(2)]
        s_pad = es.enter_context(nc.sbuf_tensor("s_pad", [128, 4096], BF16))
        guT = es.enter_context(nc.sbuf_tensor("guT", [128, 32 * 72], BF16))
        bTbuf = es.enter_context(nc.sbuf_tensor("bTbuf", [128, 4096], BF16))
        s_gb = es.enter_context(nc.sbuf_tensor("s_gb", [128, 8], F32))
        s_iot = es.enter_context(nc.sbuf_tensor("s_iot", [128, 128], F32))
        s_id = es.enter_context(nc.sbuf_tensor("s_id", [128, 128], BF16))
        s_g = es.enter_context(nc.sbuf_tensor("s_g", [128, 8], F32))
        s_mx = es.enter_context(nc.sbuf_tensor("s_mx", [128, 1], F32))
        s_e = es.enter_context(nc.sbuf_tensor("s_e", [128, 4], F32))
        s_not = es.enter_context(nc.sbuf_tensor("s_not", [128, 1], F32))
        s_msk = es.enter_context(nc.sbuf_tensor("s_msk", [128, 4], F32))
        s_u = es.enter_context(nc.sbuf_tensor("s_u", [128, 64], F32))
        umbuf = es.enter_context(nc.sbuf_tensor("umbuf", [128, 4 * 64], BF16))
        umTall = es.enter_context(nc.sbuf_tensor("umTall", [128, 8 * 512], BF16))
        s_yt = [es.enter_context(nc.sbuf_tensor(f"s_yt{i}", [128, 512], F32))
                for i in range(2)]
        s_yst = [es.enter_context(nc.sbuf_tensor(f"s_yst{i}", [128, 512], F32))
                 for i in range(2)]
        p_t = [es.enter_context(nc.psum_tensor(f"p_t{i}", [128, 512], BF16)) for i in range(2)]
        p_g = es.enter_context(nc.psum_tensor("p_g", [128, 72], F32))
        p_f = [es.enter_context(nc.psum_tensor(f"p_f{i}", [128, 512], F32)) for i in range(2)]
        sm = sems(es, nc, ["xld", "setld", "ident", "setT", "setTe", "gmm", "gev",
                           "vt", "umsk", "umTp", "umTe", "wld0", "wld1", "wT", "wTe",
                           "fmm", "yld0", "yld1", "fev", "yst0", "yst1"])
        wld2 = [sm["wld0"], sm["wld1"]]
        yld2 = [sm["yld0"], sm["yld1"]]
        yst2 = [sm["yst0"], sm["yst1"]]

        with nc.Block() as blk:
            emit_identity(blk, s_iot, s_id, sm["ident"])

            @blk.sync
            def _(sync):
                sm["xld"].inc(sync.dma_start(
                    out=xnbuf[:].rearrange("p (c t) -> p c t", t=1024),
                    in_=xn_dram.rearrange("(c p) t -> p c t", p=128)), 16)
                sm["setld"].inc(sync.dma_start(out=s_gb[:], in_=gb_dram), 16)

            @blk.vector
            def _(v):
                v.memset(s_pad[:], 0.0)
                v.engine_nop().then_inc(sm["setld"].h, 1)
        sm["setld"].n += 1

        with nc.Block() as blk:
            @blk.gpsimd
            def _(g):
                sm["setld"].inc(g.dma_start(out=s_pad[0:64, :], in_=a_dram), 16)
                sm["setld"].inc(g.dma_start(out=s_pad[64:68, :], in_=gw_dram), 16)

            @blk.tensor
            def _(t):
                t.wait_ge(sm["ident"].h, 2)
                t.wait_ge(sm["setld"].h, sm["setld"].n)
                for c4 in range(8):
                    if c4 >= 2:
                        t.wait_ge(sm["setTe"].h, c4 - 1)
                    for j in range(4):
                        c = 4 * c4 + j
                        inst = t.transpose(p_t[c4 % 2][:, 128 * j:128 * (j + 1)],
                                           s_pad[:, 128 * c:128 * (c + 1)], s_id[:])
                    sm["setT"].inc(inst)

            @blk.vector
            def _(v):
                for c4 in range(8):
                    v.wait_ge(sm["setT"].h, c4 + 1)
                    for j in range(4):
                        c = 4 * c4 + j
                        v.tensor_copy(guT[:, 72 * c + 8:72 * c + 72],
                                      p_t[c4 % 2][:, 128 * j:128 * j + 64])
                    sm["setTe"].inc(v.tensor_copy(
                        guT[:].rearrange("p (c e) -> p c e", e=72)[
                            :, 4 * c4:4 * c4 + 4, 0:8],
                        p_t[c4 % 2][:].rearrange("p (c e) -> p c e", e=128)[
                            :, :, 64:72]))

        with nc.Block() as blk:
            @blk.vector
            def _(v):
                sm["setTe"].inc(v.memset(s_pad[:, 0:256], 0.0))

            @blk.gpsimd
            def _(g):
                for c2 in range(16):
                    g.wait_ge(sm["setTe"].h, 1)
                    if c2 >= 1:
                        g.wait_ge(sm["setT"].h, 8 + c2)
                    sm["setld"].inc(g.dma_start(
                        out=s_pad[:].rearrange("p (a e) -> p a e", e=128)[:, 0:2, 0:64],
                        in_=b_dram.rearrange("(a p) e -> p a e", p=128)[
                            :, 2 * c2:2 * c2 + 2, :]), 16)

            @blk.tensor
            def _(t):
                for c2 in range(16):
                    t.wait_ge(sm["setld"].h, sm["setld"].n - 16 * (15 - c2))
                    if c2 >= 1:
                        t.wait_ge(sm["setTe"].h, 1 + c2)
                    for j in range(2):
                        inst = t.transpose(p_t[c2 % 2][:, 128 * j:128 * (j + 1)],
                                           s_pad[:, 128 * j:128 * (j + 1)], s_id[:])
                    sm["setT"].inc(inst)

            @blk.vector
            def _(v):
                for c2 in range(16):
                    v.wait_ge(sm["setT"].h, 8 + c2 + 1)
                    for j in range(2):
                        inst = v.tensor_copy(
                            bTbuf[0:64, 128 * (2 * c2 + j):128 * (2 * c2 + j + 1)],
                            p_t[c2 % 2][0:64, 128 * j:128 * (j + 1)])
                    sm["setTe"].inc(inst)

        with nc.Block() as blk:
            @blk.tensor
            def _(t):
                t.wait_ge(sm["xld"].h, 16)
                for tt in range(NT):
                    if tt >= 1:
                        t.wait_ge(sm["gev"].h, 2 * tt)
                    for c in range(32):
                        inst = t.matmul(
                            p_g[:],
                            xnbuf[:, 1024 * c + 128 * tt:1024 * c + 128 * (tt + 1)],
                            guT[:, 72 * c:72 * (c + 1)],
                            start=(c == 0), stop=(c == 31))
                    sm["gmm"].inc(inst)
                    t.wait_ge(sm["umsk"].h, tt + 1)
                    for r in range(4):
                        m = 4 * tt + r
                        if m >= 2:
                            t.wait_ge(sm["umTe"].h, m - 1)
                        sm["umTp"].inc(t.transpose(p_t[m % 2][0:64, 0:128],
                                                   umbuf[:, 64 * r:64 * (r + 1)],
                                                   s_id[:]))

            @blk.vector
            def _(v):
                vt = sm["vt"]
                for tt in range(NT):
                    v.wait_ge(sm["gmm"].h, tt + 1)
                    v.wait_ge(sm["setld"].h, sm["setld"].n)
                    tick(v, vt, v.tensor_add(s_g[:], p_g[:, 0:8], s_gb[:]))
                    tick(v, vt, v.reduce_max(s_mx[:], s_g[:], axis=AX))
                    tick(v, vt, v.tensor_scalar(s_e[:], s_g[:, 0:4], s_mx[:], None,
                                                ALU.is_equal))
                    tick(v, vt, v.tensor_copy(s_msk[:, 0:1], s_e[:, 0:1]))
                    tick(v, vt, v.tensor_scalar(s_not[:], s_e[:, 0:1], -1.0, 1.0,
                                                ALU.mult, ALU.add))
                    tick(v, vt, v.tensor_mul(s_msk[:, 1:2], s_e[:, 1:2], s_not[:]))
                    tick(v, vt, v.scalar_tensor_tensor(
                        s_not[:], s_msk[:, 1:2], -1.0, s_not[:], ALU.mult, ALU.add))
                    tick(v, vt, v.tensor_mul(s_msk[:, 2:3], s_e[:, 2:3], s_not[:]))
                    tick(v, vt, v.scalar_tensor_tensor(
                        s_not[:], s_msk[:, 2:3], -1.0, s_not[:], ALU.mult, ALU.add))
                    tick(v, vt, v.tensor_mul(s_msk[:, 3:4], s_e[:, 3:4], s_not[:]))
                    tick(v, vt, v.tensor_copy(s_u[:], p_g[:, 8:72]))
                    v.engine_nop().then_inc(sm["gev"].h, 2)
                    sm["gev"].n += 2
                    if tt >= 1:
                        v.wait_ge(sm["umTp"].h, 4 * tt)
                    tick(v, vt, v.memset(umbuf[:], 0.0))
                    for r, R in enumerate(RANKS):
                        tick(v, vt, v.tensor_scalar(
                            umbuf[:, 64 * r:64 * r + R], s_u[:, 0:R],
                            s_msk[:, r:r + 1], None, ALU.mult))
                    v.engine_nop().then_inc(sm["umsk"].h, 1)
                    sm["umsk"].n += 1
                    for r in range(4):
                        m = 4 * tt + r
                        v.wait_ge(sm["umTp"].h, m + 1)
                        sm["umTe"].inc(v.tensor_copy(
                            umTall[0:64, 512 * tt + 128 * r:512 * tt + 128 * (r + 1)],
                            p_t[m % 2][0:64, 0:128]))

        with nc.Block() as blk:
            @blk.gpsimd
            def _(g):
                for og in range(8):
                    for a in range(4):
                        i = og * 4 + a
                        if i >= 2:
                            g.wait_ge(sm["wT"].h, 8 * (i - 1))
                        src = w_dram[512 * og + 128 * a:512 * og + 128 * (a + 1), :]
                        wld2[i % 2].inc(g.dma_start(out=s_w[i % 2][:], in_=src), 16)

            @blk.tensor
            def _(t):
                for og in range(8):
                    for a in range(4):
                        i = og * 4 + a
                        t.wait_ge(wld2[i % 2].h, 16 * (i // 2 + 1))
                        for c4 in range(8):
                            gi = i * 8 + c4
                            if gi >= 2:
                                t.wait_ge(sm["wTe"].h, gi - 1)
                            for j in range(4):
                                c = 4 * c4 + j
                                inst = t.transpose(p_t[gi % 2][:, 128 * j:128 * (j + 1)],
                                                   s_w[i % 2][:, 128 * c:128 * (c + 1)],
                                                   s_id[:])
                            sm["wT"].inc(inst)
                    t.wait_ge(sm["wTe"].h, (og + 1) * 32)
                    for tt in range(NT):
                        k = og * NT + tt
                        if k >= 2:
                            t.wait_ge(sm["fev"].h, k - 1)
                        for c in range(32):
                            t.matmul(p_f[k % 2][:],
                                     xnbuf[:, 1024 * c + 128 * tt:1024 * c + 128 * (tt + 1)],
                                     wTbuf[:, 512 * c:512 * (c + 1)],
                                     start=(c == 0), stop=False)
                        for r in range(4):
                            inst = t.matmul(p_f[k % 2][:],
                                            umTall[0:64, 512 * tt + 128 * r:
                                                   512 * tt + 128 * (r + 1)],
                                            bTbuf[0:64, 512 * og:512 * (og + 1)],
                                            start=False, stop=(r == 3))
                        sm["fmm"].inc(inst)

            @blk.vector
            def _(v):
                for og in range(8):
                    for a in range(4):
                        i = og * 4 + a
                        for c4 in range(8):
                            gi = i * 8 + c4
                            v.wait_ge(sm["wT"].h, gi + 1)
                            dst = wTbuf[:].rearrange("p (c o) -> p c o", o=512)[
                                :, 4 * c4:4 * c4 + 4, 128 * a:128 * (a + 1)]
                            sm["wTe"].inc(v.tensor_copy(
                                dst, p_t[gi % 2][:].rearrange("p (c o) -> p c o", o=128)))
                    for tt in range(NT):
                        k = og * NT + tt
                        v.wait_ge(sm["fmm"].h, k + 1)
                        v.wait_ge(yld2[k % 2].h, 16 * (k // 2 + 1))
                        if k >= 2:
                            v.wait_ge(yst2[k % 2].h, 16 * (k // 2))
                        sm["fev"].inc(v.tensor_add(s_yst[k % 2][:], p_f[k % 2][:],
                                                   s_yt[k % 2][:]))

            @blk.sync
            def _(sync):
                for og in range(8):
                    for tt in range(NT):
                        k = og * NT + tt
                        if k >= 2:
                            sync.wait_ge(sm["fev"].h, k - 1)
                        yld2[k % 2].inc(sync.dma_start(
                            out=s_yt[k % 2][:],
                            in_=y_in_dram[128 * tt:128 * (tt + 1),
                                          512 * og:512 * (og + 1)]), 16)

            @blk.scalar
            def _(s):
                for og in range(8):
                    for tt in range(NT):
                        k = og * NT + tt
                        s.wait_ge(sm["fev"].h, k + 1)
                        yst2[k % 2].inc(s.dma_start(
                            out=y_out_dram[128 * tt:128 * (tt + 1),
                                           512 * og:512 * (og + 1)],
                            in_=s_yst[k % 2][:]), 16)
                s.wait_ge(yst2[0].h, yst2[0].n)
                s.wait_ge(yst2[1].h, yst2[1].n)


def build_ffn0():
    nc = bass.Bass("TRN2", target_bir_lowering=False,
                   detect_race_conditions=(os.environ.get("SIM_NORACE") != "1"))
    xn = nc.dram_tensor("xn", [D, TPC], BF16, kind="ExternalInput").ap()
    w = nc.dram_tensor("w", [D, D], F32, kind="ExternalInput").ap()
    a = nc.dram_tensor("a", [64, D], F32, kind="ExternalInput").ap()
    b = nc.dram_tensor("b", [D, 64], F32, kind="ExternalInput").ap()
    gw = nc.dram_tensor("gw", [4, D], F32, kind="ExternalInput").ap()
    gb = nc.dram_tensor("gb", [128, 8], F32, kind="ExternalInput").ap()
    y_in = nc.dram_tensor("y_in", [TPC, D], F32, kind="ExternalInput").ap()
    y_out = nc.dram_tensor("y_out", [TPC, D], F32, kind="ExternalOutput").ap()
    emit_ffn0(nc, xn, w, a, b, gw, gb, y_in, y_out)
    return nc



def emit_dequant(nc, jobs):
    T = [float(x) for x in NF4]
    es = ExitStack()
    with es:
        W0 = 4096
        qbf = [es.enter_context(nc.sbuf_tensor(f"dq_q{i}", [128, W0], BF16)) for i in range(2)]
        ssc = [es.enter_context(nc.sbuf_tensor(f"dq_s{i}", [128, W0 // 64], F32)) for i in range(2)]
        tg = es.enter_context(nc.sbuf_tensor("dq_g", [128, W0], BF16))
        th = es.enter_context(nc.sbuf_tensor("dq_h", [128, W0], BF16))
        tb0 = es.enter_context(nc.sbuf_tensor("dq_b0", [128, W0], BF16))
        pA = es.enter_context(nc.sbuf_tensor("dq_pA", [128, W0], BF16))
        pB = es.enter_context(nc.sbuf_tensor("dq_pB", [128, W0], BF16))
        z = [es.enter_context(nc.sbuf_tensor(f"dq_z{i}", [128, W0], BF16)) for i in range(4)]
        wout = es.enter_context(nc.sbuf_tensor("dq_w", [128, W0], BF16))
        u1 = es.enter_context(nc.sbuf_tensor("dq_u1", [128, W0], U8))
        u2 = es.enter_context(nc.sbuf_tensor("dq_u2", [128, W0], U8))
        u3 = es.enter_context(nc.sbuf_tensor("dq_u3", [128, W0], U8))
        s_iot = es.enter_context(nc.sbuf_tensor("dq_iot", [128, 128], F32))
        s_id = es.enter_context(nc.sbuf_tensor("dq_id", [128, 128], BF16))
        stg = [es.enter_context(nc.sbuf_tensor(f"dq_st{i}", [128, 512], BF16)) for i in range(2)]
        p_t = [es.enter_context(nc.psum_tensor(f"dq_pt{i}", [128, 512], BF16)) for i in range(2)]
        sm = sems(es, nc, ["ident", "qld0", "qld1", "lut", "tp", "tev", "out0", "out1"])
        qld2 = [sm["qld0"], sm["qld1"]]
        out2 = [sm["out0"], sm["out1"]]

        pieces = []
        for ji, (qd, scd, od) in enumerate(jobs):
            R = qd.tensor.shape[0] if hasattr(qd, "tensor") else None
            R = qd.shape[0]
            I = qd.shape[1]
            offs = []
            o = 0
            while o < I:
                w = min(4096, I - o)
                offs.append((o, w))
                o += w
            for tt in range(R // 128):
                for (o, w) in offs:
                    pieces.append((ji, tt, o, w))

        with nc.Block() as blk:
            emit_identity(blk, s_iot, s_id, sm["ident"])

            @blk.gpsimd
            def _(g):
                for i, (ji, tt, o, w) in enumerate(pieces):
                    qd, scd, od = jobs[ji]
                    if i >= 2:
                        g.wait_ge(sm["lut"].h, 2 * (i - 1))
                    qld2[i % 2].inc(g.dma_start(
                        out=qbf[i % 2][:, 0:w],
                        in_=qd[128 * tt:128 * (tt + 1), o:o + w]), 16)
                    qld2[i % 2].inc(g.dma_start(
                        out=ssc[i % 2][:, 0:w // 64],
                        in_=scd[128 * tt:128 * (tt + 1), o // 64:(o + w) // 64]), 16)

            @blk.vector
            def _(v):
                for i, (ji, tt, o, w) in enumerate(pieces):
                    v.wait_ge(qld2[i % 2].h, 32 * (i // 2 + 1))
                    f = qbf[i % 2][:, 0:w]
                    v.tensor_scalar(u3[:, 0:w], f, 8.0, None, ALU.is_ge)
                    v.tensor_scalar(tg[:, 0:w], f, 8.0, None, ALU.is_ge)
                    v.scalar_tensor_tensor(tg[:, 0:w], tg[:, 0:w], -8.0, f,
                                           ALU.mult, ALU.add)
                    v.tensor_scalar(u2[:, 0:w], tg[:, 0:w], 4.0, None, ALU.is_ge)
                    v.tensor_scalar(th[:, 0:w], tg[:, 0:w], 4.0, None, ALU.is_ge)
                    v.scalar_tensor_tensor(th[:, 0:w], th[:, 0:w], -4.0, tg[:, 0:w],
                                           ALU.mult, ALU.add)
                    v.tensor_scalar(u1[:, 0:w], th[:, 0:w], 2.0, None, ALU.is_ge)
                    v.tensor_scalar(tb0[:, 0:w], th[:, 0:w], 2.0, None, ALU.is_ge)
                    v.scalar_tensor_tensor(tb0[:, 0:w], tb0[:, 0:w], -2.0, th[:, 0:w],
                                           ALU.mult, ALU.add)
                    for j in range(4):
                        v.tensor_scalar(pA[:, 0:w], tb0[:, 0:w],
                                        T[4 * j + 1] - T[4 * j], T[4 * j],
                                        ALU.mult, ALU.add)
                        v.tensor_scalar(pB[:, 0:w], tb0[:, 0:w],
                                        T[4 * j + 3] - T[4 * j + 2], T[4 * j + 2],
                                        ALU.mult, ALU.add)
                        v.select(z[j][:, 0:w], u1[:, 0:w], pB[:, 0:w], pA[:, 0:w])
                    v.select(z[0][:, 0:w], u2[:, 0:w], z[1][:, 0:w], z[0][:, 0:w])
                    v.select(z[2][:, 0:w], u2[:, 0:w], z[3][:, 0:w], z[2][:, 0:w])
                    v.select(z[0][:, 0:w], u3[:, 0:w], z[2][:, 0:w], z[0][:, 0:w])
                    if i >= 1:
                        v.wait_ge(sm["tp"].h, sum(
                            pieces[k][3] // 128 for k in range(i)))
                    inst = v.tensor_tensor(
                        wout[:, 0:w].rearrange("p (b j) -> p b j", j=64),
                        z[0][:, 0:w].rearrange("p (b j) -> p b j", j=64),
                        ssc[i % 2][:, 0:w // 64].rearrange(
                            "p (b j) -> p b j", j=1).broadcast_to([128, w // 64, 64]),
                        ALU.mult)
                    sm["lut"].inc(inst, 2)

            @blk.tensor
            def _(t):
                t.wait_ge(sm["ident"].h, 2)
                gcnt = 0
                for i, (ji, tt, o, w) in enumerate(pieces):
                    t.wait_ge(sm["lut"].h, 2 * (i + 1))
                    nch = w // 128
                    for c4 in range((nch + 3) // 4):
                        n4 = min(4, nch - 4 * c4)
                        if gcnt >= 2:
                            t.wait_ge(sm["tev"].h, gcnt - 1)
                        for j in range(n4):
                            c = 4 * c4 + j
                            inst = t.transpose(p_t[gcnt % 2][:, 128 * j:128 * (j + 1)],
                                               wout[:, 128 * c:128 * (c + 1)], s_id[:])
                        sm["tp"].inc(inst, n4)
                        gcnt += 1

            @blk.vector
            def _(v):
                pass

            @blk.scalar
            def _(s):
                gcnt = 0
                tpcnt = 0
                for i, (ji, tt, o, w) in enumerate(pieces):
                    qd, scd, od = jobs[ji]
                    nch = w // 128
                    for c4 in range((nch + 3) // 4):
                        n4 = min(4, nch - 4 * c4)
                        tpcnt += n4
                        s.wait_ge(sm["tp"].h, tpcnt)
                        if gcnt >= 2:
                            s.wait_ge(out2[gcnt % 2].h, 16 * (gcnt // 2))
                        sm["tev"].inc(s.activation(
                            stg[gcnt % 2][:, 0:128 * n4], p_t[gcnt % 2][:, 0:128 * n4],
                            AF.Identity))
                        s.wait_ge(sm["tev"].h, sm["tev"].n)
                        dst = od.rearrange("(a p) t -> p a t", p=128)[
                            :, (o // 128) + 4 * c4:(o // 128) + 4 * c4 + n4,
                            128 * tt:128 * (tt + 1)]
                        src = stg[gcnt % 2][:, 0:128 * n4].rearrange("p (a t) -> p a t", t=128)
                        out2[gcnt % 2].inc(s.dma_start(out=dst, in_=src), 16)
                        gcnt += 1
                s.wait_ge(out2[0].h, out2[0].n)
                s.wait_ge(out2[1].h, out2[1].n)


def build_dequant(nt12=11, nt3=4):
    nc = bass.Bass("TRN2", target_bir_lowering=False,
                   detect_race_conditions=(os.environ.get("SIM_NORACE") != "1"))
    H = 11008
    q1 = nc.dram_tensor("q1", [128 * nt12, D], I32, kind="ExternalInput").ap()
    s1 = nc.dram_tensor("s1", [128 * nt12, D // 64], F32, kind="ExternalInput").ap()
    q2 = nc.dram_tensor("q2", [128 * nt12, D], I32, kind="ExternalInput").ap()
    s2 = nc.dram_tensor("s2", [128 * nt12, D // 64], F32, kind="ExternalInput").ap()
    q3 = nc.dram_tensor("q3", [128 * nt3, H], I32, kind="ExternalInput").ap()
    s3 = nc.dram_tensor("s3", [128 * nt3, H // 64], F32, kind="ExternalInput").ap()
    w1t = nc.dram_tensor("w1t", [D, 128 * nt12], BF16, kind="ExternalOutput").ap()
    w2t = nc.dram_tensor("w2t", [D, 128 * nt12], BF16, kind="ExternalOutput").ap()
    w3t = nc.dram_tensor("w3t", [H, 128 * nt3], BF16, kind="ExternalOutput").ap()
    emit_dequant(nc, [(q1, s1, w1t), (q2, s2, w2t), (q3, s3, w3t)])
    return nc



def emit_ffn1(nc, xn_dram, w1t_dram, w2t_dram, w3t_dram, y_in_dram, y_out_dram):
    H = 11008
    NHB = H // 128
    es = ExitStack()
    with es:
        xnbuf = es.enter_context(nc.sbuf_tensor("xnbuf", [128, 32 * 1024], BF16))
        hbuf = es.enter_context(nc.sbuf_tensor("hbuf", [128, NHB * 512], BF16))
        w1s = [es.enter_context(nc.sbuf_tensor(f"w1s{i}", [128, 32 * 128], BF16))
               for i in range(2)]
        w2s = [es.enter_context(nc.sbuf_tensor(f"w2s{i}", [128, 32 * 128], BF16))
               for i in range(2)]
        w3s = [es.enter_context(nc.sbuf_tensor(f"w3s{i}", [128, 512], BF16))
               for i in range(2)]
        s_sil = es.enter_context(nc.sbuf_tensor("s_sil", [128, 512], BF16))
        s_yt = [es.enter_context(nc.sbuf_tensor(f"s_yt{i}", [128, 512], F32))
                for i in range(2)]
        s_yst = [es.enter_context(nc.sbuf_tensor(f"s_yst{i}", [128, 512], F32))
                 for i in range(2)]
        p_1 = es.enter_context(nc.psum_tensor("p_1", [128, 512], F32))
        p_2 = es.enter_context(nc.psum_tensor("p_2", [128, 512], F32))
        p_o = [es.enter_context(nc.psum_tensor(f"p_o{i}", [128, 512], F32))
               for i in range(4)]
        sm = sems(es, nc, ["xld", "w1l0", "w1l1", "w2l0", "w2l1", "w3l0", "w3l1",
                           "mm1", "mm2", "sil", "hev", "omm", "yld0", "yld1",
                           "oev", "yst0", "yst1"])
        w1l = [sm["w1l0"], sm["w1l1"]]
        w2l = [sm["w2l0"], sm["w2l1"]]
        w3l = [sm["w3l0"], sm["w3l1"]]
        yld2 = [sm["yld0"], sm["yld1"]]
        yst2 = [sm["yst0"], sm["yst1"]]

        with nc.Block() as blk:
            @blk.sync
            def _(sync):
                sm["xld"].inc(sync.dma_start(
                    out=xnbuf[:].rearrange("p (c t) -> p c t", t=1024),
                    in_=xn_dram.rearrange("(c p) t -> p c t", p=128)), 16)

        for tg in range(2):
            with nc.Block() as blk:
                @blk.sync
                def _(sync):
                    for hb in range(NHB):
                        i = tg * NHB + hb
                        if i >= 2:
                            sync.wait_ge(sm["mm1"].h, i - 1)
                        src = w1t_dram.rearrange("(c p) n -> p c n", p=128)[
                            :, :, 128 * hb:128 * (hb + 1)]
                        w1l[hb % 2].inc(sync.dma_start(
                            out=w1s[hb % 2][:].rearrange("p (c m) -> p c m", m=128),
                            in_=src), 16)

                @blk.gpsimd
                def _(g):
                    for hb in range(NHB):
                        i = tg * NHB + hb
                        if i >= 2:
                            g.wait_ge(sm["mm2"].h, i - 1)
                        src = w2t_dram.rearrange("(c p) n -> p c n", p=128)[
                            :, :, 128 * hb:128 * (hb + 1)]
                        w2l[hb % 2].inc(g.dma_start(
                            out=w2s[hb % 2][:].rearrange("p (c m) -> p c m", m=128),
                            in_=src), 16)

                @blk.tensor
                def _(t):
                    t.wait_ge(sm["xld"].h, 16)
                    for hb in range(NHB):
                        i = tg * NHB + hb
                        t.wait_ge(w1l[hb % 2].h, 16 * (tg * NHB // 2 + hb // 2 + 1))
                        t.wait_ge(w2l[hb % 2].h, 16 * (tg * NHB // 2 + hb // 2 + 1))
                        if i >= 1:
                            t.wait_ge(sm["sil"].h, i)
                            t.wait_ge(sm["hev"].h, i)
                        for c in range(32):
                            inst = t.matmul(
                                p_1[:],
                                w1s[hb % 2][:, 128 * c:128 * (c + 1)],
                                xnbuf[:, 1024 * c + 512 * tg:1024 * c + 512 * (tg + 1)],
                                start=(c == 0), stop=(c == 31))
                        sm["mm1"].inc(inst)
                        for c in range(32):
                            inst = t.matmul(
                                p_2[:],
                                w2s[hb % 2][:, 128 * c:128 * (c + 1)],
                                xnbuf[:, 1024 * c + 512 * tg:1024 * c + 512 * (tg + 1)],
                                start=(c == 0), stop=(c == 31))
                        sm["mm2"].inc(inst)

                @blk.scalar
                def _(s):
                    for hb in range(NHB):
                        i = tg * NHB + hb
                        s.wait_ge(sm["mm1"].h, i + 1)
                        if i >= 1:
                            s.wait_ge(sm["hev"].h, i)
                        sm["sil"].inc(s.activation(s_sil[:], p_1[:], AF.Silu))

                @blk.vector
                def _(v):
                    for hb in range(NHB):
                        i = tg * NHB + hb
                        v.wait_ge(sm["sil"].h, i + 1)
                        v.wait_ge(sm["mm2"].h, i + 1)
                        sm["hev"].inc(v.tensor_mul(
                            hbuf[:, 512 * hb:512 * (hb + 1)], s_sil[:], p_2[:]))

            with nc.Block() as blk:
                @blk.sync
                def _(sync):
                    for og in range(8):
                        for hb in range(NHB):
                            i = (tg * 8 + og) * NHB + hb
                            if i >= 2:
                                sync.wait_ge(sm["omm"].h, 4 * (i - 1))
                            w3l[hb % 2].inc(sync.dma_start(
                                out=w3s[hb % 2][:],
                                in_=w3t_dram[128 * hb:128 * (hb + 1),
                                             512 * og:512 * (og + 1)]), 16)

                @blk.tensor
                def _(t):
                    for og in range(8):
                        t.wait_ge(sm["oev"].h, (tg * 8 + og) * 4)
                        for hb in range(NHB):
                            i = (tg * 8 + og) * NHB + hb
                            t.wait_ge(w3l[hb % 2].h,
                                      16 * ((tg * 8 + og) * NHB // 2 + hb // 2 + 1))
                            for ts in range(4):
                                inst = t.matmul(
                                    p_o[ts][:],
                                    hbuf[:, 512 * hb + 128 * ts:512 * hb + 128 * (ts + 1)],
                                    w3s[hb % 2][:],
                                    start=(hb == 0), stop=(hb == NHB - 1))
                            sm["omm"].inc(inst, 4)

                @blk.gpsimd
                def _(g):
                    for og in range(8):
                        for ts in range(4):
                            i = (tg * 8 + og) * 4 + ts
                            if i >= 2:
                                g.wait_ge(sm["oev"].h, i - 1)
                            yld2[i % 2].inc(g.dma_start(
                                out=s_yt[i % 2][:],
                                in_=y_in_dram[512 * tg + 128 * ts:512 * tg + 128 * (ts + 1),
                                              512 * og:512 * (og + 1)]), 16)

                @blk.vector
                def _(v):
                    for og in range(8):
                        v.wait_ge(sm["omm"].h, 4 * (tg * 8 + og + 1) * NHB)
                        for ts in range(4):
                            i = (tg * 8 + og) * 4 + ts
                            v.wait_ge(yld2[i % 2].h, 16 * (i // 2 + 1))
                            if i >= 2:
                                v.wait_ge(yst2[i % 2].h, 16 * (i // 2))
                            sm["oev"].inc(v.tensor_add(s_yst[i % 2][:], p_o[ts][:],
                                                       s_yt[i % 2][:]))

                @blk.scalar
                def _(s):
                    for og in range(8):
                        for ts in range(4):
                            i = (tg * 8 + og) * 4 + ts
                            s.wait_ge(sm["oev"].h, i + 1)
                            yst2[i % 2].inc(s.dma_start(
                                out=y_out_dram[512 * tg + 128 * ts:
                                               512 * tg + 128 * (ts + 1),
                                               512 * og:512 * (og + 1)],
                                in_=s_yst[i % 2][:]), 16)
        with nc.Block() as blk:
            @blk.scalar
            def _(s):
                s.wait_ge(yst2[0].h, yst2[0].n)
                s.wait_ge(yst2[1].h, yst2[1].n)


def build_ffn1():
    nc = bass.Bass("TRN2", target_bir_lowering=False,
                   detect_race_conditions=(os.environ.get("SIM_NORACE") != "1"))
    H = 11008
    xn = nc.dram_tensor("xn", [D, TPC], BF16, kind="ExternalInput").ap()
    w1t = nc.dram_tensor("w1t", [D, H], BF16, kind="ExternalInput").ap()
    w2t = nc.dram_tensor("w2t", [D, H], BF16, kind="ExternalInput").ap()
    w3t = nc.dram_tensor("w3t", [H, D], BF16, kind="ExternalInput").ap()
    y_in = nc.dram_tensor("y_in", [TPC, D], F32, kind="ExternalInput").ap()
    y_out = nc.dram_tensor("y_out", [TPC, D], F32, kind="ExternalOutput").ap()
    emit_ffn1(nc, xn, w1t, w2t, w3t, y_in, y_out)
    return nc



def build_ln_qkv0():
    nc = bass.Bass("TRN2", target_bir_lowering=False)
    x = nc.dram_tensor("x", [TPC, D], F32, kind="ExternalInput").ap()
    wqkv = nc.dram_tensor("wqkv", [D, 3072], F32, kind="ExternalInput").ap()
    bq_pt = nc.dram_tensor("bq_pt", [128, 8], F32, kind="ExternalInput").ap()
    bk_pt = nc.dram_tensor("bk_pt", [128, 8], F32, kind="ExternalInput").ap()
    bv_bc = nc.dram_tensor("bv_bc", [128, 1024], F32, kind="ExternalInput").ap()
    xn_fm = nc.dram_tensor("xn_fm", [D, TPC], BF16, kind="ExternalOutput").ap()
    q_fm = nc.dram_tensor("q_fm", [1024, TPC], BF16, kind="ExternalOutput").ap()
    k_fm = nc.dram_tensor("k_fm", [1024, TPC], BF16, kind="ExternalOutput").ap()
    v_tm = nc.dram_tensor("v_tm", [TPC, 1024], BF16, kind="ExternalOutput").ap()
    projs = [
        dict(w=wqkv, col0=0, dout=1024, bias=bq_pt, sbias0=0, out=q_fm, kind="fm"),
        dict(w=wqkv, col0=1024, dout=1024, bias=bk_pt, sbias0=8, out=k_fm, kind="fm"),
        dict(w=wqkv, col0=2048, dout=1024, bias=bv_bc, sbias0=0, out=v_tm, kind="tm"),
    ]
    emit_ln_proj(nc, x, xn_fm, projs)
    return nc


def build_ln_qkv1():
    nc = bass.Bass("TRN2", target_bir_lowering=False)
    x = nc.dram_tensor("x", [TPC, D], F32, kind="ExternalInput").ap()
    wq = nc.dram_tensor("wq", [D, D], F32, kind="ExternalInput").ap()
    wkv = nc.dram_tensor("wkv", [D, 256], F32, kind="ExternalInput").ap()
    xn_fm = nc.dram_tensor("xn_fm", [D, TPC], BF16, kind="ExternalOutput").ap()
    q_fm = nc.dram_tensor("q_fm", [D, TPC], BF16, kind="ExternalOutput").ap()
    k_fm = nc.dram_tensor("k_fm", [128, TPC], BF16, kind="ExternalOutput").ap()
    v_tm = nc.dram_tensor("v_tm", [TPC, 128], BF16, kind="ExternalOutput").ap()
    projs = [
        dict(w=wq, col0=0, dout=4096, bias=None, sbias0=0, out=q_fm, kind="fm"),
        dict(w=wkv, col0=0, dout=128, bias=None, sbias0=0, out=k_fm, kind="fm"),
        dict(w=wkv, col0=128, dout=128, bias=None, sbias0=0, out=v_tm, kind="tm"),
    ]
    emit_ln_proj(nc, x, xn_fm, projs)
    return nc



def own_token_rows(core):
    b, j = divmod(core, 4)
    tiles = [j + 4 * t for t in range(NT)]
    return np.concatenate([np.arange(128 * g, 128 * (g + 1)) for g in tiles])


def _bc(v, n):
    return np.ascontiguousarray(np.broadcast_to(np.asarray(v, np.float32), (128, n)))


def kernel(L0_attn_x, L0_attn_Wqkv, L0_attn_bqkv, L0_attn_Wo, L0_attn_bo,
           L0_ffn_W, L0_ffn_b, L0_ffn_A, L0_ffn_B, L0_ffn_gate_w, L0_ffn_gate_b,
           L1_attn_Wq, L1_attn_Wkv, L1_attn_Wo, L1_ffn_w1_q, L1_ffn_w1_scale,
           L1_ffn_w2_q, L1_ffn_w2_scale, L1_ffn_w3_q, L1_ffn_w3_scale):
    H = 11008
    x = np.ascontiguousarray(np.asarray(L0_attn_x, np.float32))
    wqkv = np.ascontiguousarray(np.asarray(L0_attn_Wqkv, np.float32))
    bqkv = np.asarray(L0_attn_bqkv, np.float32)
    wo0 = np.ascontiguousarray(np.asarray(L0_attn_Wo, np.float32))
    bo0 = np.asarray(L0_attn_bo, np.float32)
    w0 = np.ascontiguousarray(np.asarray(L0_ffn_W, np.float32))
    b0f = np.asarray(L0_ffn_b, np.float32)
    a0 = np.ascontiguousarray(np.asarray(L0_ffn_A, np.float32))
    b0m = np.ascontiguousarray(np.asarray(L0_ffn_B, np.float32))
    gw0 = np.ascontiguousarray(np.asarray(L0_ffn_gate_w, np.float32))
    gb0 = np.asarray(L0_ffn_gate_b, np.float32)
    wq1 = np.ascontiguousarray(np.asarray(L1_attn_Wq, np.float32))
    wkv1 = np.ascontiguousarray(np.asarray(L1_attn_Wkv, np.float32))
    wo1 = np.ascontiguousarray(np.asarray(L1_attn_Wo, np.float32))

    rows = [own_token_rows(c) for c in range(NCORE)]
    x_own = [np.ascontiguousarray(x[c // 4][rows[c]]) for c in range(NCORE)]

    bq_pt = np.ascontiguousarray(bqkv[0:1024].reshape(8, 128).T)
    bk_pt = np.ascontiguousarray(bqkv[1024:2048].reshape(8, 128).T)
    bv_bc = _bc(bqkv[2048:3072], 1024)
    bo_bc = _bc(bo0, D)
    bffn_bc = _bc(b0f, D)
    gb_bc = np.full((128, 8), -1e30, np.float32)
    gb_bc[:, 0:4] = gb0
    gb_bc = np.ascontiguousarray(gb_bc)

    def assemble_seq(parts, axis):
        full = [None, None]
        for b in range(B):
            gshape = list(parts[4 * b].shape)
            gshape[axis] = S
            g = np.empty(gshape, parts[0].dtype)
            for j in range(4):
                p = parts[4 * b + j]
                for t in range(NT):
                    gsl = [slice(None)] * p.ndim
                    lsl = [slice(None)] * p.ndim
                    gsl[axis] = slice(128 * (4 * t + j), 128 * (4 * t + j + 1))
                    lsl[axis] = slice(128 * t, 128 * (t + 1))
                    g[tuple(gsl)] = p[tuple(lsl)]
            full[b] = g
        return full

    nc = build_ln_qkv0()
    maps = [dict(x=x_own[c], wqkv=wqkv, bq_pt=bq_pt, bk_pt=bk_pt, bv_bc=bv_bc)
            for c in range(NCORE)]
    r1 = run_bass_kernel_spmd(nc, maps, core_ids=list(range(NCORE))).results
    xn0 = [r["xn_fm"] for r in r1]
    k0_full = assemble_seq([r["k_fm"] for r in r1], axis=1)
    v0_full = assemble_seq([r["v_tm"] for r in r1], axis=0)

    nc = build_attn0()
    maps = [dict(q=r1[c]["q_fm"], k=np.ascontiguousarray(k0_full[c // 4]),
                 v=np.ascontiguousarray(v0_full[c // 4]), wo=wo0, x=x_own[c],
                 bias1=bo_bc, bias2=bffn_bc) for c in range(NCORE)]
    r2 = run_bass_kernel_spmd(nc, maps, core_ids=list(range(NCORE))).results

    nc = build_ffn0()
    maps = [dict(xn=xn0[c], w=w0, a=a0, b=b0m, gw=gw0, gb=gb_bc,
                 y_in=r2[c]["y"]) for c in range(NCORE)]
    r3 = run_bass_kernel_spmd(nc, maps, core_ids=list(range(NCORE))).results

    nc = build_ln_qkv1()
    maps = [dict(x=r3[c]["y_out"], wq=wq1, wkv=wkv1) for c in range(NCORE)]
    r4 = run_bass_kernel_spmd(nc, maps, core_ids=list(range(NCORE))).results
    xn1 = [r["xn_fm"] for r in r4]
    k1_full = assemble_seq([r["k_fm"] for r in r4], axis=1)
    v1_full = assemble_seq([r["v_tm"] for r in r4], axis=0)

    i_idx = np.arange(128)[:, None]
    jj_idx = np.arange(512)[None, :]
    masks = []
    for c in range(NCORE):
        j = c % 4
        md = np.where(jj_idx - i_idx - 128 * j <= 0, -128.0 * j, -1e30)
        masks.append(np.ascontiguousarray(np.concatenate(
            [md, np.full((128, 512), -128.0 * j)], axis=1).astype(np.float32)))
    nc = build_attn1_core()
    maps = [dict(q=r4[c]["q_fm"], k=np.ascontiguousarray(k1_full[c // 4]),
                 v=np.ascontiguousarray(v1_full[c // 4]), msk=masks[c])
            for c in range(NCORE)]
    r5 = run_bass_kernel_spmd(nc, maps, core_ids=list(range(NCORE))).results

    nc = build_proj_res()
    maps = [dict(act=r5[c]["attnfm"], w=wo1, y_in=r3[c]["y_out"])
            for c in range(NCORE)]
    r6 = run_bass_kernel_spmd(nc, maps, core_ids=list(range(NCORE))).results

    def shard12(qm, sc, c):
        idx = np.minimum(np.arange(11 * c, 11 * c + 11), 85)
        qs = np.concatenate([qm[128 * g:128 * (g + 1)] for g in idx])
        ss = np.concatenate([sc[128 * g:128 * (g + 1)] for g in idx])
        return np.ascontiguousarray(qs), np.ascontiguousarray(ss)

    q1m = np.asarray(L1_ffn_w1_q, np.int32)
    s1m = np.asarray(L1_ffn_w1_scale, np.float32)
    q2m = np.asarray(L1_ffn_w2_q, np.int32)
    s2m = np.asarray(L1_ffn_w2_scale, np.float32)
    q3m = np.asarray(L1_ffn_w3_q, np.int32)
    s3m = np.asarray(L1_ffn_w3_scale, np.float32)
    nc = build_dequant()
    maps = []
    for c in range(NCORE):
        qa, sa = shard12(q1m, s1m, c)
        qb, sb = shard12(q2m, s2m, c)
        qc = np.ascontiguousarray(q3m[512 * c:512 * (c + 1)])
        scc = np.ascontiguousarray(s3m[512 * c:512 * (c + 1)])
        maps.append(dict(q1=qa, s1=sa, q2=qb, s2=sb, q3=qc, s3=scc))
    r7 = run_bass_kernel_spmd(nc, maps, core_ids=list(range(NCORE))).results
    w1t = np.empty((D, H), BF)
    w2t = np.empty((D, H), BF)
    for c in range(NCORE):
        for k in range(11):
            g = 11 * c + k
            if g <= 85:
                w1t[:, 128 * g:128 * (g + 1)] = r7[c]["w1t"][:, 128 * k:128 * (k + 1)]
                w2t[:, 128 * g:128 * (g + 1)] = r7[c]["w2t"][:, 128 * k:128 * (k + 1)]
    w3t = np.concatenate([r7[c]["w3t"] for c in range(NCORE)], axis=1)
    w1t = np.ascontiguousarray(w1t)
    w2t = np.ascontiguousarray(w2t)
    w3t = np.ascontiguousarray(w3t)

    nc = build_ffn1()
    maps = [dict(xn=xn1[c], w1t=w1t, w2t=w2t, w3t=w3t, y_in=r6[c]["y_out"])
            for c in range(NCORE)]
    r8 = run_bass_kernel_spmd(nc, maps, core_ids=list(range(NCORE))).results

    out = np.empty((B, S, D), np.float32)
    for c in range(NCORE):
        out[c // 4][rows[c]] = r8[c]["y_out"]
    return out


# version 02
# speedup vs baseline: 1.0123x; 1.0123x over previous
import os
import numpy as np
import ml_dtypes
import concourse.bass as bass
import concourse.mybir as mybir
from concourse.bass_utils import run_bass_kernel_spmd
from contextlib import ExitStack

F32 = mybir.dt.float32
BF16 = mybir.dt.bfloat16
I32 = mybir.dt.int32
U8 = mybir.dt.uint8
AF = mybir.ActivationFunctionType
ALU = mybir.AluOpType
AX = mybir.AxisListType.X

BF = ml_dtypes.bfloat16

NF4 = [-1.0, -0.6961928009986877, -0.5250730514526367, -0.39491748809814453,
       -0.28444138169288635, -0.18477343022823334, -0.09105003625154495, 0.0,
       0.07958029955625534, 0.16093020141124725, 0.24611230194568634,
       0.33791524171829224, 0.44070982933044434, 0.5626170039176941,
       0.7229568362236023, 1.0]

D = 4096
S = 4096
B = 2
TPC = 1024
NT = TPC // 128
NCORE = 8


_UID = [0]


def _sbt(nc, name, shape, dtype):
    _UID[0] += 1
    return nc.sbuf_tensor(f"{name}_u{_UID[0]}", shape, dtype)


def _pst(nc, name, shape, dtype):
    _UID[0] += 1
    return nc.psum_tensor(f"{name}_u{_UID[0]}", shape, dtype)


class Sem:

    def __init__(self, h):
        self.h = h
        self.n = 0

    def inc(self, inst, k=1):
        inst.then_inc(self.h, k)
        self.n += k


def sems(es, nc, names):
    out = {}
    for n in names:
        _UID[0] += 1
        out[n] = Sem(es.enter_context(nc.semaphore(f"{n}_u{_UID[0]}")))
    with nc.Block() as blk:
        @blk.sync
        def _(sync):
            for s in out.values():
                sync.sem_clear(s.h)
    return out


def tick(eng, sem, inst):
    sem.inc(inst)
    eng.wait_ge(sem.h, sem.n)


def emit_identity(blk, s_iot, s_id, sem):
    @blk.gpsimd
    def _(g):
        sem.inc(g.iota(s_iot[:], [[1, 128]], channel_multiplier=-1,
                       allow_small_or_imprecise_dtypes=True))

    @blk.vector
    def _(v):
        v.wait_ge(sem.h, sem.n)
        sem.inc(v.tensor_scalar(s_id[:], s_iot[:], 0.0, None, ALU.is_equal))



def emit_ln_proj(nc, x_dram, xn_fm_dram, projs):
    es = ExitStack()
    with es:
        xnbuf = es.enter_context(_sbt(nc, "xnbuf", [128, 32 * 1024], BF16))
        s_x = [es.enter_context(_sbt(nc, f"s_x{i}", [128, 4096], F32)) for i in range(2)]
        s_xn = es.enter_context(_sbt(nc, "s_xn", [128, 4096], BF16))
        s_junk = es.enter_context(_sbt(nc, "s_junk", [128, 4096], F32))
        s_iot = es.enter_context(_sbt(nc, "s_iot", [128, 128], F32))
        s_id = es.enter_context(_sbt(nc, "s_id", [128, 128], BF16))
        s_ssq = [es.enter_context(_sbt(nc, f"s_ssq{i}", [128, 1], F32)) for i in range(2)]
        s_sum = [es.enter_context(_sbt(nc, f"s_sum{i}", [128, 1], F32)) for i in range(2)]
        s_mean = [es.enter_context(_sbt(nc, f"s_mean{i}", [128, 1], F32)) for i in range(2)]
        s_var = [es.enter_context(_sbt(nc, f"s_var{i}", [128, 1], F32)) for i in range(2)]
        s_varraw = [es.enter_context(_sbt(nc, f"s_varraw{i}", [128, 1], F32)) for i in range(2)]
        s_rstd = es.enter_context(_sbt(nc, "s_rstd", [128, 1], F32))
        s_tmp1 = es.enter_context(_sbt(nc, "s_tmp1", [128, 1], F32))
        p_t = [es.enter_context(_pst(nc, f"p_t{i}", [128, 512], BF16)) for i in range(2)]
        sm = sems(es, nc, ["ld0", "ld1", "ident", "sq", "var", "sqrt", "norm", "tp",
                           "tev", "xout", "vt"])
        ld2 = [sm["ld0"], sm["ld1"]]

        with nc.Block() as blk:
            emit_identity(blk, s_iot, s_id, sm["ident"])

            @blk.sync
            def _(sync):
                for tt in range(NT):
                    if tt >= 2:
                        sync.wait_ge(sm["norm"].h, tt - 1)
                    ld2[tt % 2].inc(sync.dma_start(
                        out=s_x[tt % 2][:], in_=x_dram[128 * tt:128 * (tt + 1), :]), 16)

            @blk.scalar
            def _(s):
                for tt in range(NT):
                    s.wait_ge(ld2[tt % 2].h, 16 * (tt // 2 + 1))
                    if tt >= 2:
                        s.wait_ge(sm["var"].h, 2 * (tt - 1))
                    sm["sq"].inc(s.activation(s_junk[:], s_x[tt % 2][:], AF.Square,
                                              accum_out=s_ssq[tt % 2][:]))
                    s.wait_ge(sm["var"].h, 2 * (tt + 1))
                    sm["sqrt"].inc(s.activation(s_var[tt % 2][:], s_varraw[tt % 2][:],
                                                AF.Sqrt))

            @blk.vector
            def _(v):
                vt = sm["vt"]
                for tt in range(NT):
                    v.wait_ge(ld2[tt % 2].h, 16 * (tt // 2 + 1))
                    tick(v, vt, v.reduce_sum(s_sum[tt % 2][:], s_x[tt % 2][:], axis=AX))
                    v.wait_ge(sm["sq"].h, tt + 1)
                    tick(v, vt, v.tensor_scalar(s_mean[tt % 2][:], s_sum[tt % 2][:],
                                                1.0 / D, None, ALU.mult))
                    tick(v, vt, v.tensor_scalar(s_varraw[tt % 2][:], s_ssq[tt % 2][:],
                                                1.0 / D, None, ALU.mult))
                    tick(v, vt, v.tensor_mul(s_tmp1[:], s_mean[tt % 2][:], s_mean[tt % 2][:]))
                    tick(v, vt, v.tensor_sub(s_varraw[tt % 2][:], s_varraw[tt % 2][:], s_tmp1[:]))
                    sm["var"].inc(v.tensor_scalar(s_varraw[tt % 2][:], s_varraw[tt % 2][:],
                                                  1e-5, None, ALU.add), 2)
                    v.wait_ge(sm["sqrt"].h, tt + 1)
                    tick(v, vt, v.reciprocal(s_rstd[:], s_var[tt % 2][:]))
                    tick(v, vt, v.tensor_mul(s_tmp1[:], s_rstd[:], s_rstd[:]))
                    tick(v, vt, v.tensor_mul(s_tmp1[:], s_tmp1[:], s_varraw[tt % 2][:]))
                    tick(v, vt, v.tensor_scalar(s_tmp1[:], s_tmp1[:], -0.5, 1.5,
                                                ALU.mult, ALU.add))
                    tick(v, vt, v.tensor_mul(s_rstd[:], s_rstd[:], s_tmp1[:]))
                    if tt >= 1:
                        v.wait_ge(sm["tp"].h, 8 * tt)
                    sm["norm"].inc(v.tensor_scalar(
                        s_xn[:], s_x[tt % 2][:], s_mean[tt % 2][:], s_rstd[:],
                        ALU.subtract, ALU.mult))
                    for c4 in range(8):
                        g = tt * 8 + c4
                        v.wait_ge(sm["tp"].h, g + 1)
                        dst = xnbuf[:].rearrange("p (c t) -> p c t", t=1024)[
                            :, 4 * c4:4 * c4 + 4, 128 * tt:128 * (tt + 1)]
                        sm["tev"].inc(v.tensor_copy(
                            dst, p_t[c4 % 2][:].rearrange("p (c t) -> p c t", t=128)))

            @blk.tensor
            def _(t):
                t.wait_ge(sm["ident"].h, 2)
                for tt in range(NT):
                    t.wait_ge(sm["norm"].h, tt + 1)
                    for c4 in range(8):
                        g = tt * 8 + c4
                        if g >= 2:
                            t.wait_ge(sm["tev"].h, g - 1)
                        for j in range(4):
                            c = 4 * c4 + j
                            inst = t.transpose(p_t[c4 % 2][:, 128 * j:128 * (j + 1)],
                                               s_xn[:, 128 * c:128 * (c + 1)], s_id[:])
                        sm["tp"].inc(inst)

            @blk.sync
            def _(sync):
                for tt in range(NT):
                    for c4 in range(8):
                        g = tt * 8 + c4
                        sync.wait_ge(sm["tev"].h, g + 1)
                        dst = xn_fm_dram.rearrange("(a p) (b t) -> a p b t", p=128, t=128)[
                            4 * c4:4 * c4 + 4, :, tt, :].rearrange("a p t -> p a t")
                        src = xnbuf[:].rearrange("p (c t) -> p c t", t=1024)[
                            :, 4 * c4:4 * c4 + 4, 128 * tt:128 * (tt + 1)]
                        sm["xout"].inc(sync.dma_start(out=dst, in_=src), 16)
                sync.wait_ge(sm["xout"].h, sm["xout"].n)

        fm_tiles = []
        tm_grps = []
        for pi, pr in enumerate(projs):
            if pr["kind"] == "fm":
                for dt in range(pr["dout"] // 128):
                    fm_tiles.append((pi, dt))
            else:
                for gidx in range((pr["dout"] + 511) // 512):
                    tm_grps.append((pi, gidx))

        wslab = [es.enter_context(_sbt(nc, f"wslab{i}", [128, 32 * 128], BF16))
                 for i in range(2)]
        vslab = [es.enter_context(_sbt(nc, f"vslab{i}", [128, 32 * 512], BF16))
                 for i in range(2)]
        s_bias = es.enter_context(_sbt(nc, "s_bias", [128, 32], F32))
        s_vbias = es.enter_context(_sbt(nc, "s_vbias", [128, 1024], F32))
        s_stage = [es.enter_context(_sbt(nc, f"s_stage{i}", [128, 512], BF16))
                   for i in range(2)]
        p_mm = [es.enter_context(_pst(nc, f"p_mm{i}", [128, 512], F32)) for i in range(2)]
        sm2 = sems(es, nc, ["wld0", "wld1", "vld0", "vld1", "bld", "mm",
                            "ev", "evd0", "evd1"])
        wld2 = [sm2["wld0"], sm2["wld1"]]
        vld2 = [sm2["vld0"], sm2["vld1"]]
        evd2 = [sm2["evd0"], sm2["evd1"]]

        with nc.Block() as blk:
            nfm = 2 * len(fm_tiles)

            @blk.gpsimd
            def _(g):
                for pi, pr in enumerate(projs):
                    if pr["bias"] is not None and pr["kind"] == "fm":
                        ntile = pr["dout"] // 128
                        sm2["bld"].inc(g.dma_start(
                            out=s_bias[:, pr["sbias0"]:pr["sbias0"] + ntile],
                            in_=pr["bias"]), 16)
                    if pr["bias"] is not None and pr["kind"] == "tm":
                        sm2["bld"].inc(g.dma_start(
                            out=s_vbias[:, 0:pr["dout"]], in_=pr["bias"]), 16)
                for i, (pi, dt) in enumerate(fm_tiles):
                    pr = projs[pi]
                    if i >= 2:
                        g.wait_ge(sm2["mm"].h, 2 * (i - 1))
                    src = pr["w"].rearrange("(c p) n -> p c n", p=128)[
                        :, :, pr["col0"] + 128 * dt: pr["col0"] + 128 * (dt + 1)]
                    wld2[i % 2].inc(g.dma_start(out=wslab[i % 2][:].rearrange(
                        "p (c m) -> p c m", m=128), in_=src), 16)
                for i, (pi, gidx) in enumerate(tm_grps):
                    pr = projs[pi]
                    dw = min(512, pr["dout"] - 512 * gidx)
                    if i >= 2:
                        g.wait_ge(sm2["mm"].h, nfm + NT * (i - 1))
                    src = pr["w"].rearrange("(c p) n -> p c n", p=128)[
                        :, :, pr["col0"] + 512 * gidx: pr["col0"] + 512 * gidx + dw]
                    vld2[i % 2].inc(g.dma_start(out=vslab[i % 2][:].rearrange(
                        "p (c m) -> p c m", m=512)[:, :, 0:dw], in_=src), 16)

            @blk.tensor
            def _(t):
                for i, (pi, dt) in enumerate(fm_tiles):
                    t.wait_ge(wld2[i % 2].h, 16 * (i // 2 + 1))
                    for tg in range(2):
                        k = 2 * i + tg
                        if k >= 2:
                            t.wait_ge(sm2["ev"].h, k - 1)
                        for c in range(32):
                            inst = t.matmul(p_mm[k % 2][:],
                                            wslab[i % 2][:, 128 * c:128 * (c + 1)],
                                            xnbuf[:, 1024 * c + 512 * tg:1024 * c + 512 * (tg + 1)],
                                            start=(c == 0), stop=(c == 31))
                        sm2["mm"].inc(inst)
                for i, (pi, gidx) in enumerate(tm_grps):
                    pr = projs[pi]
                    dw = min(512, pr["dout"] - 512 * gidx)
                    t.wait_ge(vld2[i % 2].h, 16 * (i // 2 + 1))
                    for tt in range(NT):
                        k = nfm + i * NT + tt
                        if k >= 2:
                            t.wait_ge(sm2["ev"].h, k - 1)
                        for c in range(32):
                            inst = t.matmul(p_mm[k % 2][:, 0:dw],
                                            xnbuf[:, 1024 * c + 128 * tt:1024 * c + 128 * (tt + 1)],
                                            vslab[i % 2][:, 512 * c:512 * c + dw],
                                            start=(c == 0), stop=(c == 31))
                        sm2["mm"].inc(inst)

            @blk.scalar
            def _(s):
                s.wait_ge(sm2["bld"].h, sm2["bld"].n)
                for i, (pi, dt) in enumerate(fm_tiles):
                    pr = projs[pi]
                    for tg in range(2):
                        k = 2 * i + tg
                        s.wait_ge(sm2["mm"].h, k + 1)
                        if k >= 2:
                            s.wait_ge(evd2[k % 2].h, 16 * (k // 2))
                        if pr["bias"] is not None:
                            bias_ap = s_bias[:, pr["sbias0"] + dt:pr["sbias0"] + dt + 1]
                        else:
                            bias_ap = 0.0
                        sm2["ev"].inc(s.activation(s_stage[k % 2][:], p_mm[k % 2][:],
                                                   AF.Identity, bias=bias_ap))

            @blk.vector
            def _(v):
                for i, (pi, gidx) in enumerate(tm_grps):
                    pr = projs[pi]
                    dw = min(512, pr["dout"] - 512 * gidx)
                    v.wait_ge(sm2["bld"].h, sm2["bld"].n)
                    for tt in range(NT):
                        k = nfm + i * NT + tt
                        v.wait_ge(sm2["mm"].h, k + 1)
                        if k >= 2:
                            v.wait_ge(evd2[k % 2].h, 16 * (k // 2))
                        if pr["bias"] is not None:
                            sm2["ev"].inc(v.tensor_add(
                                s_stage[k % 2][:, 0:dw], p_mm[k % 2][:, 0:dw],
                                s_vbias[:, 512 * gidx:512 * gidx + dw]))
                        else:
                            sm2["ev"].inc(v.tensor_copy(
                                s_stage[k % 2][:, 0:dw], p_mm[k % 2][:, 0:dw]))

            @blk.sync
            def _(sync):
                for i, (pi, dt) in enumerate(fm_tiles):
                    pr = projs[pi]
                    for tg in range(2):
                        k = 2 * i + tg
                        sync.wait_ge(sm2["ev"].h, k + 1)
                        evd2[k % 2].inc(sync.dma_start(
                            out=pr["out"][128 * dt:128 * (dt + 1), 512 * tg:512 * (tg + 1)],
                            in_=s_stage[k % 2][:]), 16)
                nfm = 2 * len(fm_tiles)
                for i, (pi, gidx) in enumerate(tm_grps):
                    pr = projs[pi]
                    dw = min(512, pr["dout"] - 512 * gidx)
                    for tt in range(NT):
                        k = nfm + i * NT + tt
                        sync.wait_ge(sm2["ev"].h, k + 1)
                        evd2[k % 2].inc(sync.dma_start(
                            out=pr["out"][128 * tt:128 * (tt + 1), 512 * gidx:512 * gidx + dw],
                            in_=s_stage[k % 2][:, 0:dw]), 16)
                sync.wait_ge(evd2[0].h, evd2[0].n)
                sync.wait_ge(evd2[1].h, evd2[1].n)



def emit_attn0(nc, q_dram, k_dram, v_dram, wo_dram, x_dram, bias_bcs, y_dram,
               k_src=None, v_src=None):
    DH = 1024
    NKB = 8
    scale = 1.0 / 32.0
    es = ExitStack()
    with es:
        qbuf = es.enter_context(_sbt(nc, "qbuf", [128, 8 * 1024], BF16))
        attnfm = es.enter_context(_sbt(nc, "attnfm", [128, 8 * 1024], BF16))
        s_bias = es.enter_context(_sbt(nc, "s_bias", [128, 4096], F32))
        s_bias2 = es.enter_context(_sbt(nc, "s_bias2", [128, 4096], F32))
        kslab = [es.enter_context(_sbt(nc, f"kslab{i}", [128, 8 * 512], BF16))
                 for i in range(2)]
        vslab = [es.enter_context(_sbt(nc, f"vslab{i}", [128, 4 * 1024], BF16))
                 for i in range(2)]
        s_row = es.enter_context(_sbt(nc, "s_row", [128, 4096], F32))
        s_probs = es.enter_context(_sbt(nc, "s_probs", [128, 4096], BF16))
        s_probsT = es.enter_context(_sbt(nc, "s_probsT", [128, 4096], BF16))
        s_m = es.enter_context(_sbt(nc, "s_m", [128, 1], F32))
        s_negm = es.enter_context(_sbt(nc, "s_negm", [128, 1], F32))
        s_esum = es.enter_context(_sbt(nc, "s_esum", [128, 1], F32))
        s_rsum = es.enter_context(_sbt(nc, "s_rsum", [128, 1], F32))
        s_iot = es.enter_context(_sbt(nc, "s_iot", [128, 128], F32))
        s_id = es.enter_context(_sbt(nc, "s_id", [128, 128], BF16))
        s_pv = es.enter_context(_sbt(nc, "s_pv", [128, 1024], BF16))
        woslab = [es.enter_context(_sbt(nc, f"woslab{i}", [128, 8 * 512], BF16))
                  for i in range(2)]
        s_xt = [es.enter_context(_sbt(nc, f"s_xt{i}", [128, 512], F32))
                for i in range(2)]
        s_yst = [es.enter_context(_sbt(nc, f"s_yst{i}", [128, 512], F32))
                 for i in range(2)]
        p_s = [es.enter_context(_pst(nc, f"p_s{i}", [128, 512], F32)) for i in range(2)]
        p_T = [es.enter_context(_pst(nc, f"p_T{i}", [128, 512], BF16)) for i in range(2)]
        p_pv = [es.enter_context(_pst(nc, f"p_pv{i}", [128, 512], F32)) for i in range(2)]
        p_o = [es.enter_context(_pst(nc, f"p_o{i}", [128, 512], F32)) for i in range(2)]
        sm = sems(es, nc, ["qld", "bld", "ident", "kld0", "kld1", "vld0", "vld1",
                           "smm", "sev", "max", "exp", "tpp", "tpe", "pvmm", "pvev",
                           "attT", "atev", "wld0", "wld1", "xld0", "xld1", "omm",
                           "oev", "yst0", "yst1", "vt", "st2"])
        kld2 = [sm["kld0"], sm["kld1"]]
        vld2 = [sm["vld0"], sm["vld1"]]
        wld2 = [sm["wld0"], sm["wld1"]]
        xld2 = [sm["xld0"], sm["xld1"]]
        yst2 = [sm["yst0"], sm["yst1"]]

        if k_src is None:
            k_src = lambda g: k_dram.rearrange("(c p) t -> p c t", p=128)[
                :, :, 128 * g:128 * (g + 1)]
        if v_src is None:
            v_src = lambda g: v_dram[128 * g:128 * (g + 1), :]

        with nc.Block() as blk:
            emit_identity(blk, s_iot, s_id, sm["ident"])

            @blk.sync
            def _(sync):
                src = q_dram.rearrange("(c p) t -> p c t", p=128)
                sm["qld"].inc(sync.dma_start(
                    out=qbuf[:].rearrange("p (c t) -> p c t", t=1024), in_=src), 16)
                for i, bb in enumerate(bias_bcs):
                    sm["bld"].inc(sync.dma_start(
                        out=(s_bias if i == 0 else s_bias2)[:], in_=bb), 16)
                sync.wait_ge(sm["bld"].h, sm["bld"].n)

            @blk.vector
            def _(v):
                v.wait_ge(sm["bld"].h, sm["bld"].n)
                if len(bias_bcs) > 1:
                    v.tensor_add(s_bias[:], s_bias[:], s_bias2[:])
                v.engine_nop().then_inc(sm["bld"].h, 1)
        sm["bld"].n += 1

        with nc.Block() as blk:
            @blk.sync
            def _(sync):
                for qt in range(NT):
                    for kb in range(NKB):
                        i = qt * NKB + kb
                        if i >= 2:
                            sync.wait_ge(sm["smm"].h, i - 1)
                        for g4 in range(4):
                            g = kb * 4 + g4
                            kld2[i % 2].inc(sync.dma_start(
                                out=kslab[i % 2][:].rearrange("p (c t) -> p c t", t=512)[
                                    :, :, 128 * g4:128 * (g4 + 1)],
                                in_=k_src(g)), 16)

            @blk.gpsimd
            def _(g_):
                for qt in range(NT):
                    for kb in range(NKB):
                        i = qt * NKB + kb
                        if i >= 2:
                            g_.wait_ge(sm["pvmm"].h, i - 1)
                        for g4 in range(4):
                            g = kb * 4 + g4
                            vld2[i % 2].inc(g_.dma_start(
                                out=vslab[i % 2][:, 1024 * g4:1024 * (g4 + 1)],
                                in_=v_src(g)), 16)

            @blk.tensor
            def _(t):
                t.wait_ge(sm["ident"].h, 2)
                t.wait_ge(sm["qld"].h, 16)
                for qt in range(NT):
                    for kb in range(NKB):
                        i = qt * NKB + kb
                        t.wait_ge(kld2[i % 2].h, 16 * 4 * (i // 2 + 1))
                        if i >= 2:
                            t.wait_ge(sm["sev"].h, i - 1)
                        for c in range(8):
                            inst = t.matmul(
                                p_s[i % 2][:],
                                qbuf[:, 1024 * c + 128 * qt:1024 * c + 128 * (qt + 1)],
                                kslab[i % 2][:, 512 * c:512 * (c + 1)],
                                start=(c == 0), stop=(c == 7))
                        sm["smm"].inc(inst)
                    t.wait_ge(sm["exp"].h, qt + 1)
                    for c4 in range(8):
                        gi = qt * 8 + c4
                        if gi >= 2:
                            t.wait_ge(sm["tpe"].h, gi - 1)
                        for j in range(4):
                            c = 4 * c4 + j
                            inst = t.transpose(p_T[c4 % 2][:, 128 * j:128 * (j + 1)],
                                               s_probs[:, 128 * c:128 * (c + 1)], s_id[:])
                        sm["tpp"].inc(inst)
                    for kb in range(NKB):
                        i = qt * NKB + kb
                        t.wait_ge(vld2[i % 2].h, 16 * 4 * (i // 2 + 1))
                        t.wait_ge(sm["tpe"].h, qt * 8 + (kb * 4 + 4 + 3) // 4)
                        for dg in range(2):
                            for kc in range(4):
                                c = kb * 4 + kc
                                inst = t.matmul(
                                    p_pv[dg][:],
                                    s_probsT[:, 128 * c:128 * (c + 1)],
                                    vslab[i % 2][:, 1024 * kc + 512 * dg:
                                                  1024 * kc + 512 * (dg + 1)],
                                    start=(kb == 0 and kc == 0),
                                    stop=(kb == NKB - 1 and kc == 3))
                        sm["pvmm"].inc(inst)
                    t.wait_ge(sm["pvev"].h, 2 * (qt + 1))
                    for pair in range(2):
                        gi = qt * 2 + pair
                        if gi >= 2:
                            t.wait_ge(sm["atev"].h, gi - 1)
                        for j in range(4):
                            c = 4 * pair + j
                            inst = t.transpose(p_T[pair % 2][:, 128 * j:128 * (j + 1)],
                                               s_pv[:, 128 * c:128 * (c + 1)], s_id[:])
                        sm["attT"].inc(inst)

            @blk.vector
            def _(v):
                vt = sm["vt"]
                for qt in range(NT):
                    for kb in range(NKB):
                        i = qt * NKB + kb
                        v.wait_ge(sm["smm"].h, i + 1)
                        sm["sev"].inc(v.tensor_scalar(
                            s_row[:, 512 * kb:512 * (kb + 1)], p_s[i % 2][:],
                            scale, None, ALU.mult))
                    tick(v, vt, v.reduce_max(s_m[:], s_row[:], axis=AX))
                    tick(v, vt, v.tensor_scalar(s_negm[:], s_m[:], -1.0, None, ALU.mult))
                    v.engine_nop().then_inc(sm["max"].h, 1)
                    sm["max"].n += 1
                    v.wait_ge(sm["exp"].h, qt + 1)
                    tick(v, vt, v.reciprocal(s_rsum[:], s_esum[:]))
                    for c4 in range(8):
                        gi = qt * 8 + c4
                        v.wait_ge(sm["tpp"].h, gi + 1)
                        dst = s_probsT[:].rearrange("p (c t) -> p c t", t=128)[
                            :, 4 * c4:4 * c4 + 4, :]
                        sm["tpe"].inc(v.tensor_copy(
                            dst, p_T[c4 % 2][:].rearrange("p (c t) -> p c t", t=128)))
                    v.wait_ge(sm["pvmm"].h, (qt + 1) * NKB)
                    for dg in range(2):
                        sm["pvev"].inc(v.tensor_scalar(
                            s_pv[:, 512 * dg:512 * (dg + 1)], p_pv[dg][:],
                            s_rsum[:], None, ALU.mult))
                    for pair in range(2):
                        gi = qt * 2 + pair
                        v.wait_ge(sm["attT"].h, gi + 1)
                        dst = attnfm[:].rearrange("p (c t) -> p c t", t=1024)[
                            :, 4 * pair:4 * pair + 4, 128 * qt:128 * (qt + 1)]
                        sm["atev"].inc(v.tensor_copy(
                            dst, p_T[pair % 2][:].rearrange("p (c t) -> p c t", t=128)))

            @blk.scalar
            def _(s):
                for qt in range(NT):
                    s.wait_ge(sm["max"].h, qt + 1)
                    if qt >= 1:
                        s.wait_ge(sm["tpp"].h, 8 * qt)
                    sm["exp"].inc(s.activation(s_probs[:], s_row[:], AF.Exp,
                                               bias=s_negm[:], accum_out=s_esum[:]))

        with nc.Block() as blk:
            @blk.gpsimd
            def _(g_):
                for og in range(8):
                    if og >= 2:
                        g_.wait_ge(sm["omm"].h, 8 * (og - 1))
                    src = wo_dram.rearrange("(c p) n -> p c n", p=128)[
                        :, :, 512 * og:512 * (og + 1)]
                    wld2[og % 2].inc(g_.dma_start(
                        out=woslab[og % 2][:].rearrange("p (c m) -> p c m", m=512),
                        in_=src), 16)

            @blk.sync
            def _(sync):
                for og in range(8):
                    for qt in range(NT):
                        i = og * NT + qt
                        if i >= 2:
                            sync.wait_ge(sm["oev"].h, i - 1)
                        xld2[i % 2].inc(sync.dma_start(
                            out=s_xt[i % 2][:],
                            in_=x_dram[128 * qt:128 * (qt + 1), 512 * og:512 * (og + 1)]), 16)

            @blk.tensor
            def _(t):
                for og in range(8):
                    t.wait_ge(wld2[og % 2].h, 16 * (og // 2 + 1))
                    for qt in range(NT):
                        i = og * NT + qt
                        if i >= 2:
                            t.wait_ge(sm["oev"].h, i - 1)
                        for c in range(8):
                            inst = t.matmul(
                                p_o[i % 2][:],
                                attnfm[:, 1024 * c + 128 * qt:1024 * c + 128 * (qt + 1)],
                                woslab[og % 2][:, 512 * c:512 * (c + 1)],
                                start=(c == 0), stop=(c == 7))
                        sm["omm"].inc(inst)

            @blk.vector
            def _(v):
                for og in range(8):
                    for qt in range(NT):
                        i = og * NT + qt
                        v.wait_ge(sm["omm"].h, i + 1)
                        v.wait_ge(xld2[i % 2].h, 16 * (i // 2 + 1))
                        if i >= 2:
                            v.wait_ge(yst2[i % 2].h, 16 * (i // 2))
                        v.tensor_add(s_yst[i % 2][:], p_o[i % 2][:],
                                     s_bias[:, 512 * og:512 * (og + 1)])
                        sm["oev"].inc(v.tensor_add(s_yst[i % 2][:], s_yst[i % 2][:],
                                                   s_xt[i % 2][:]))

            @blk.scalar
            def _(s):
                for og in range(8):
                    for qt in range(NT):
                        i = og * NT + qt
                        s.wait_ge(sm["oev"].h, i + 1)
                        yst2[i % 2].inc(s.dma_start(
                            out=y_dram[128 * qt:128 * (qt + 1), 512 * og:512 * (og + 1)],
                            in_=s_yst[i % 2][:]), 16)
                s.wait_ge(yst2[0].h, yst2[0].n)
                s.wait_ge(yst2[1].h, yst2[1].n)


def build_attn0():
    nc = bass.Bass("TRN2", target_bir_lowering=False,
                   detect_race_conditions=(os.environ.get("SIM_NORACE") != "1"))
    q = nc.dram_tensor("q", [1024, TPC], BF16, kind="ExternalInput").ap()
    k = nc.dram_tensor("k", [1024, S], BF16, kind="ExternalInput").ap()
    v = nc.dram_tensor("v", [S, 1024], BF16, kind="ExternalInput").ap()
    wo = nc.dram_tensor("wo", [1024, D], F32, kind="ExternalInput").ap()
    x = nc.dram_tensor("x", [TPC, D], F32, kind="ExternalInput").ap()
    bias1 = nc.dram_tensor("bias1", [128, D], F32, kind="ExternalInput").ap()
    bias2 = nc.dram_tensor("bias2", [128, D], F32, kind="ExternalInput").ap()
    y = nc.dram_tensor("y", [TPC, D], F32, kind="ExternalOutput").ap()
    emit_attn0(nc, q, k, v, wo, x, [bias1, bias2], y)
    return nc



NH = 32
HD = 128


def alibi_slopes(n_heads=NH):
    import math
    closest = 2 ** math.floor(math.log2(n_heads))
    base = 2.0 ** (-2.0 ** (-(math.log2(closest) - 3)))
    slopes = [base ** i for i in range(1, closest + 1)]
    if closest != n_heads:
        extra = 2.0 ** (-2.0 ** (-(math.log2(2 * closest) - 3)))
        n_extra = min(closest, n_heads - closest)
        slopes += [extra ** i for i in range(1, 2 * n_extra + 1, 2)]
    return slopes


def emit_attn1_core(nc, q_dram, k_dram, v_dram, mask_dram, attnfm_dram,
                    k_src=None, v_src=None):
    scale = float(HD) ** -0.5
    slopes = alibi_slopes()
    es = ExitStack()
    with es:
        qbuf = es.enter_context(_sbt(nc, "qbuf", [128, 32 * 1024], BF16))
        ahbuf = [es.enter_context(_sbt(nc, f"ahbuf{i}", [128, 1024], BF16))
                 for i in range(2)]
        k1buf = es.enter_context(_sbt(nc, "k1buf", [128, 4096], BF16))
        v1buf = es.enter_context(_sbt(nc, "v1buf", [128, 32 * 128], BF16))
        s_mask = es.enter_context(_sbt(nc, "s_mask", [128, 1024], F32))
        r0d = es.enter_context(_sbt(nc, "r0d", [128, 8 * 512], F32))
        rhbuf = es.enter_context(_sbt(nc, "rhbuf", [128, 8 * 512], F32))
        probs = es.enter_context(_sbt(nc, "probs", [128, 4096], BF16))
        probsT = es.enter_context(_sbt(nc, "probsT", [128, 4096], BF16))
        s_iot = es.enter_context(_sbt(nc, "s_iot", [128, 128], F32))
        s_id = es.enter_context(_sbt(nc, "s_id", [128, 128], BF16))
        s_srow = es.enter_context(_sbt(nc, "s_srow", [128, 4096], F32))
        sumbuf = es.enter_context(_sbt(nc, "sumbuf", [128, 8], F32))
        s_M = es.enter_context(_sbt(nc, "s_M", [128, 1], F32))
        s_negm = es.enter_context(_sbt(nc, "s_negm", [128, 1], F32))
        s_SS = es.enter_context(_sbt(nc, "s_SS", [128, 1], F32))
        s_rs = es.enter_context(_sbt(nc, "s_rs", [128, 1], F32))
        s_pv = es.enter_context(_sbt(nc, "s_pv", [128, 128], BF16))
        p_s = [es.enter_context(_pst(nc, f"p_s{i}", [128, 512], F32)) for i in range(2)]
        p_T = [es.enter_context(_pst(nc, f"p_T{i}", [128, 512], BF16)) for i in range(2)]
        p_pv = es.enter_context(_pst(nc, "p_pv", [128, 128], F32))
        p_pvT = es.enter_context(_pst(nc, "p_pvT", [128, 128], BF16))
        sm = sems(es, nc, ["ld", "ident", "r0", "erh", "smm", "mx", "maxd", "exp",
                           "pmul", "tpp", "tpe", "pvmm", "pvev", "pvT", "atev", "vt",
                           "sev", "ao0", "ao1"])
        ao2 = [sm["ao0"], sm["ao1"]]

        if k_src is None:
            k_src = lambda g: k_dram.rearrange("(c p) t -> p c t", p=128)[
                :, :, 128 * g:128 * (g + 1)]
        if v_src is None:
            v_src = lambda g: v_dram[128 * g:128 * (g + 1), :]

        with nc.Block() as blk:
            emit_identity(blk, s_iot, s_id, sm["ident"])

            @blk.sync
            def _(sync):
                sm["ld"].inc(sync.dma_start(
                    out=qbuf[:].rearrange("p (c t) -> p c t", t=1024),
                    in_=q_dram.rearrange("(c p) t -> p c t", p=128)), 16)
                sm["ld"].inc(sync.dma_start(out=s_mask[:], in_=mask_dram), 16)
                for g in range(32):
                    sm["ld"].inc(sync.dma_start(
                        out=k1buf[:, 128 * g:128 * (g + 1)], in_=k_src(g)), 16)
                    sm["ld"].inc(sync.dma_start(
                        out=v1buf[:, 128 * g:128 * (g + 1)], in_=v_src(g)), 16)

            @blk.gpsimd
            def _(g):
                for dlt in range(8):
                    sm["r0"].inc(g.iota(r0d[:, 512 * dlt:512 * (dlt + 1)],
                                        [[1, 512]], base=512 * (dlt - 7),
                                        channel_multiplier=-1,
                                        allow_small_or_imprecise_dtypes=True))

            @blk.vector
            def _(v):
                v.wait_ge(sm["r0"].h, 8)
                v.wait_ge(sm["ld"].h, sm["ld"].n)
                for dlt in range(7):
                    v.tensor_add(r0d[:, 512 * dlt:512 * (dlt + 1)],
                                 r0d[:, 512 * dlt:512 * (dlt + 1)],
                                 s_mask[:, 512:1024])
                sm["r0"].inc(v.tensor_add(r0d[:, 512 * 7:512 * 8],
                                          r0d[:, 512 * 7:512 * 8],
                                          s_mask[:, 0:512]))

        with nc.Block() as blk:
            @blk.scalar
            def _(s):
                s.wait_ge(sm["ld"].h, sm["ld"].n)
                s.wait_ge(sm["r0"].h, 9)
                cnt_sev = 0
                cnt_maxd = 0
                cnt_tpp = 0
                for h in range(NH):
                    if h >= 1:
                        s.wait_ge(sm["sev"].h, cnt_sev)
                    for dlt in range(8):
                        sm["erh"].inc(s.activation(
                            rhbuf[:, 512 * dlt:512 * (dlt + 1)],
                            r0d[:, 512 * dlt:512 * (dlt + 1)],
                            AF.Identity, scale=float(slopes[h])))
                    for t in range(NT):
                        cnt_sev += t + 1
                        cnt_maxd += 1
                        s.wait_ge(sm["maxd"].h, cnt_maxd)
                        if cnt_tpp:
                            s.wait_ge(sm["tpp"].h, cnt_tpp)
                        sm["exp"].inc(s.activation(
                            probs[:, 0:512 * (t + 1)], s_srow[:, 0:512 * (t + 1)],
                            AF.Exp, bias=s_negm[:], accum_out=s_SS[:]))
                        cnt_tpp += t + 1

            @blk.tensor
            def _(t_):
                t_.wait_ge(sm["ident"].h, 2)
                t_.wait_ge(sm["ld"].h, sm["ld"].n)
                cnt_tpe = 0
                cnt_exp = 0
                cnt_pvev = 0
                bidx = 0
                gi = 0
                for h in range(NH):
                    for t in range(NT):
                        g0 = h * NT + t
                        for b in range(t + 1):
                            if bidx + b >= 2:
                                t_.wait_ge(sm["sev"].h, bidx + b - 1)
                            t_.matmul(p_s[(bidx + b) % 2][:],
                                      qbuf[:, 1024 * h + 128 * t:1024 * h + 128 * (t + 1)],
                                      k1buf[:, 512 * b:512 * (b + 1)],
                                      start=True, stop=True).then_inc(sm["smm"].h, 1)
                            sm["smm"].n += 1
                        cnt_exp += 1
                        t_.wait_ge(sm["exp"].h, cnt_exp)
                        for b in range(t + 1):
                            if gi >= 2:
                                t_.wait_ge(sm["tpe"].h, gi - 1)
                            for j in range(4):
                                c = 4 * b + j
                                inst = t_.transpose(
                                    p_T[gi % 2][:, 128 * j:128 * (j + 1)],
                                    probs[:, 128 * c:128 * (c + 1)], s_id[:])
                            sm["tpp"].inc(inst)
                            gi += 1
                        for c in range(4 * (t + 1)):
                            t_.wait_ge(sm["tpe"].h, cnt_tpe + c // 4 + 1)
                            t_.matmul(p_pv[:], probsT[:, 128 * c:128 * (c + 1)],
                                      v1buf[:, 128 * c:128 * (c + 1)],
                                      start=(c == 0), stop=(c == 4 * t + 3)
                                      ).then_inc(sm["pvmm"].h, 1)
                            sm["pvmm"].n += 1
                        cnt_tpe += t + 1
                        bidx += t + 1
                        cnt_pvev += 1
                        t_.wait_ge(sm["pvev"].h, cnt_pvev)
                        if g0 >= 1:
                            t_.wait_ge(sm["atev"].h, g0)
                        sm["pvT"].inc(t_.transpose(p_pvT[:], s_pv[:], s_id[:]))

            @blk.gpsimd
            def _(g):
                for h in range(NH):
                    g.wait_ge(sm["atev"].h, 8 * (h + 1))
                    ao2[h % 2].inc(g.dma_start(
                        out=attnfm_dram[128 * h:128 * (h + 1), :],
                        in_=ahbuf[h % 2][:]), 16)

            @blk.vector
            def _(v):
                vt = sm["vt"]
                cnt_mm = 0
                cnt_exp = 0
                cnt_tpp = 0
                cnt_pvmm = 0
                cnt_pvT = 0
                bidx = 0
                for h in range(NH):
                    v.wait_ge(sm["erh"].h, 8 * (h + 1))
                    for t in range(NT):
                        g0 = h * NT + t
                        for b in range(t + 1):
                            v.wait_ge(sm["smm"].h, cnt_mm + b + 1)
                            dlt = b - t + 7
                            sm["sev"].inc(v.scalar_tensor_tensor(
                                s_srow[:, 512 * b:512 * (b + 1)],
                                p_s[(bidx + b) % 2][:], scale,
                                rhbuf[:, 512 * dlt:512 * (dlt + 1)],
                                ALU.mult, ALU.add))
                        v.wait_ge(sm["sev"].h, sm["sev"].n)
                        tick(v, vt, v.reduce_max(s_M[:], s_srow[:, 0:512 * (t + 1)],
                                                 axis=AX))
                        tick(v, vt, v.tensor_scalar(s_negm[:], s_M[:], -1.0, None,
                                                    ALU.mult))
                        v.engine_nop().then_inc(sm["maxd"].h, 1)
                        sm["maxd"].n += 1
                        cnt_exp += 1
                        v.wait_ge(sm["exp"].h, cnt_exp)
                        tick(v, vt, v.reciprocal(s_rs[:], s_SS[:]))
                        for b in range(t + 1):
                            v.wait_ge(sm["tpp"].h, cnt_tpp + b + 1)
                            dst = probsT[:].rearrange("p (c t2) -> p c t2", t2=128)[
                                :, 4 * b:4 * b + 4, :]
                            sm["tpe"].inc(v.tensor_copy(
                                dst, p_T[(cnt_tpp + b) % 2][:].rearrange(
                                    "p (c t2) -> p c t2", t2=128)))
                        cnt_pvmm += 4 * (t + 1)
                        v.wait_ge(sm["pvmm"].h, cnt_pvmm)
                        sm["pvev"].inc(v.tensor_scalar(
                            s_pv[:], p_pv[:], s_rs[:], None, ALU.mult))
                        cnt_pvT += 1
                        v.wait_ge(sm["pvT"].h, cnt_pvT)
                        if t == 0 and h >= 2:
                            v.wait_ge(ao2[h % 2].h, 16 * (h // 2))
                        sm["atev"].inc(v.tensor_copy(
                            ahbuf[h % 2][:, 128 * t:128 * (t + 1)], p_pvT[:]))
                        cnt_tpp += t + 1
                        cnt_mm += t + 1
                        bidx += t + 1

        with nc.Block() as blk:
            @blk.sync
            def _(sync):
                sync.wait_ge(ao2[0].h, ao2[0].n)
                sync.wait_ge(ao2[1].h, ao2[1].n)


def emit_proj_res(nc, act_dram, w_dram, y_in_dram, y_out_dram, nchunk=32):
    es = ExitStack()
    with es:
        actbuf = es.enter_context(_sbt(nc, "actbuf", [128, nchunk * 1024], BF16))
        woslab = [es.enter_context(_sbt(nc, f"woslab{i}", [128, nchunk * 512], BF16))
                  for i in range(2)]
        s_yt = [es.enter_context(_sbt(nc, f"s_yt{i}", [128, 512], F32))
                for i in range(2)]
        s_yst = [es.enter_context(_sbt(nc, f"s_yst{i}", [128, 512], F32))
                 for i in range(2)]
        p_o = [es.enter_context(_pst(nc, f"p_o{i}", [128, 512], F32)) for i in range(2)]
        sm = sems(es, nc, ["ald", "wld0", "wld1", "yld0", "yld1", "omm", "oev",
                           "yst0", "yst1"])
        wld2 = [sm["wld0"], sm["wld1"]]
        yld2 = [sm["yld0"], sm["yld1"]]
        yst2 = [sm["yst0"], sm["yst1"]]

        with nc.Block() as blk:
            @blk.sync
            def _(sync):
                sm["ald"].inc(sync.dma_start(
                    out=actbuf[:].rearrange("p (c t) -> p c t", t=1024),
                    in_=act_dram.rearrange("(c p) t -> p c t", p=128)), 16)

            @blk.gpsimd
            def _(g):
                for og in range(8):
                    if og >= 2:
                        g.wait_ge(sm["omm"].h, 8 * (og - 1))
                    src = w_dram.rearrange("(c p) n -> p c n", p=128)[
                        :, :, 512 * og:512 * (og + 1)]
                    wld2[og % 2].inc(g.dma_start(
                        out=woslab[og % 2][:].rearrange("p (c m) -> p c m", m=512),
                        in_=src), 16)

            @blk.tensor
            def _(t):
                t.wait_ge(sm["ald"].h, 16)
                for og in range(8):
                    t.wait_ge(wld2[og % 2].h, 16 * (og // 2 + 1))
                    for qt in range(NT):
                        i = og * NT + qt
                        if i >= 2:
                            t.wait_ge(sm["oev"].h, i - 1)
                        for c in range(nchunk):
                            inst = t.matmul(
                                p_o[i % 2][:],
                                actbuf[:, 1024 * c + 128 * qt:1024 * c + 128 * (qt + 1)],
                                woslab[og % 2][:, 512 * c:512 * (c + 1)],
                                start=(c == 0), stop=(c == nchunk - 1))
                        sm["omm"].inc(inst)

            @blk.sync
            def _(sync):
                for og in range(8):
                    for qt in range(NT):
                        i = og * NT + qt
                        if i >= 2:
                            sync.wait_ge(sm["oev"].h, i - 1)
                        yld2[i % 2].inc(sync.dma_start(
                            out=s_yt[i % 2][:],
                            in_=y_in_dram[128 * qt:128 * (qt + 1),
                                          512 * og:512 * (og + 1)]), 16)

            @blk.vector
            def _(v):
                for og in range(8):
                    for qt in range(NT):
                        i = og * NT + qt
                        v.wait_ge(sm["omm"].h, i + 1)
                        v.wait_ge(yld2[i % 2].h, 16 * (i // 2 + 1))
                        if i >= 2:
                            v.wait_ge(yst2[i % 2].h, 16 * (i // 2))
                        sm["oev"].inc(v.tensor_add(s_yst[i % 2][:], p_o[i % 2][:],
                                                   s_yt[i % 2][:]))

            @blk.scalar
            def _(s):
                for og in range(8):
                    for qt in range(NT):
                        i = og * NT + qt
                        s.wait_ge(sm["oev"].h, i + 1)
                        yst2[i % 2].inc(s.dma_start(
                            out=y_out_dram[128 * qt:128 * (qt + 1),
                                           512 * og:512 * (og + 1)],
                            in_=s_yst[i % 2][:]), 16)
                s.wait_ge(yst2[0].h, yst2[0].n)
                s.wait_ge(yst2[1].h, yst2[1].n)


def build_attn1_core(with_map=False):
    nc = bass.Bass("TRN2", target_bir_lowering=False,
                   detect_race_conditions=(os.environ.get("SIM_NORACE") != "1"))
    q = nc.dram_tensor("q", [D, TPC], BF16, kind="ExternalInput").ap()
    k = nc.dram_tensor("k", [128, S], BF16, kind="ExternalInput").ap()
    v = nc.dram_tensor("v", [S, 128], BF16, kind="ExternalInput").ap()
    msk = nc.dram_tensor("msk", [128, 1024], F32, kind="ExternalInput").ap()
    attnfm = nc.dram_tensor("attnfm", [D, TPC], BF16, kind="ExternalOutput").ap()
    emit_attn1_core(nc, q, k, v, msk, attnfm)
    return nc


def build_proj_res():
    nc = bass.Bass("TRN2", target_bir_lowering=False,
                   detect_race_conditions=(os.environ.get("SIM_NORACE") != "1"))
    act = nc.dram_tensor("act", [D, TPC], BF16, kind="ExternalInput").ap()
    w = nc.dram_tensor("w", [D, D], F32, kind="ExternalInput").ap()
    y_in = nc.dram_tensor("y_in", [TPC, D], F32, kind="ExternalInput").ap()
    y_out = nc.dram_tensor("y_out", [TPC, D], F32, kind="ExternalOutput").ap()
    emit_proj_res(nc, act, w, y_in, y_out)
    return nc



RANKS = (8, 16, 32, 64)


def emit_ffn0(nc, xn_dram, w_dram, a_dram, b_dram, gw_dram, gb_dram, y_in_dram,
              y_out_dram):
    es = ExitStack()
    with es:
        xnbuf = es.enter_context(_sbt(nc, "xnbuf", [128, 32 * 1024], BF16))
        wTbuf = es.enter_context(_sbt(nc, "wTbuf", [128, 32 * 512], BF16))
        s_w = [es.enter_context(_sbt(nc, f"s_w{i}", [128, 4096], BF16))
               for i in range(2)]
        s_pad = es.enter_context(_sbt(nc, "s_pad", [128, 4096], BF16))
        guT = es.enter_context(_sbt(nc, "guT", [128, 32 * 72], BF16))
        bTbuf = es.enter_context(_sbt(nc, "bTbuf", [128, 4096], BF16))
        s_gb = es.enter_context(_sbt(nc, "s_gb", [128, 8], F32))
        s_iot = es.enter_context(_sbt(nc, "s_iot", [128, 128], F32))
        s_id = es.enter_context(_sbt(nc, "s_id", [128, 128], BF16))
        s_g = es.enter_context(_sbt(nc, "s_g", [128, 8], F32))
        s_mx = es.enter_context(_sbt(nc, "s_mx", [128, 1], F32))
        s_e = es.enter_context(_sbt(nc, "s_e", [128, 4], F32))
        s_not = es.enter_context(_sbt(nc, "s_not", [128, 1], F32))
        s_msk = es.enter_context(_sbt(nc, "s_msk", [128, 4], F32))
        s_u = es.enter_context(_sbt(nc, "s_u", [128, 64], F32))
        umbuf = es.enter_context(_sbt(nc, "umbuf", [128, 4 * 64], BF16))
        umTall = es.enter_context(_sbt(nc, "umTall", [128, 8 * 512], BF16))
        s_yt = [es.enter_context(_sbt(nc, f"s_yt{i}", [128, 512], F32))
                for i in range(2)]
        s_yst = [es.enter_context(_sbt(nc, f"s_yst{i}", [128, 512], F32))
                 for i in range(2)]
        p_t = [es.enter_context(_pst(nc, f"p_t{i}", [128, 512], BF16)) for i in range(2)]
        p_g = es.enter_context(_pst(nc, "p_g", [128, 72], F32))
        p_f = [es.enter_context(_pst(nc, f"p_f{i}", [128, 512], F32)) for i in range(2)]
        sm = sems(es, nc, ["xld", "setld", "ident", "setT", "setTe", "gmm", "gev",
                           "vt", "umsk", "umTp", "umTe", "wld0", "wld1", "wT", "wTe",
                           "fmm", "yld0", "yld1", "fev", "yst0", "yst1"])
        wld2 = [sm["wld0"], sm["wld1"]]
        yld2 = [sm["yld0"], sm["yld1"]]
        yst2 = [sm["yst0"], sm["yst1"]]

        with nc.Block() as blk:
            emit_identity(blk, s_iot, s_id, sm["ident"])

            @blk.sync
            def _(sync):
                sm["xld"].inc(sync.dma_start(
                    out=xnbuf[:].rearrange("p (c t) -> p c t", t=1024),
                    in_=xn_dram.rearrange("(c p) t -> p c t", p=128)), 16)
                sm["setld"].inc(sync.dma_start(out=s_gb[:], in_=gb_dram), 16)

            @blk.vector
            def _(v):
                v.memset(s_pad[:], 0.0)
                v.engine_nop().then_inc(sm["setld"].h, 1)
        sm["setld"].n += 1

        with nc.Block() as blk:
            @blk.gpsimd
            def _(g):
                sm["setld"].inc(g.dma_start(out=s_pad[0:64, :], in_=a_dram), 16)
                sm["setld"].inc(g.dma_start(out=s_pad[64:68, :], in_=gw_dram), 16)

            @blk.tensor
            def _(t):
                t.wait_ge(sm["ident"].h, 2)
                t.wait_ge(sm["setld"].h, sm["setld"].n)
                for c4 in range(8):
                    if c4 >= 2:
                        t.wait_ge(sm["setTe"].h, c4 - 1)
                    for j in range(4):
                        c = 4 * c4 + j
                        inst = t.transpose(p_t[c4 % 2][:, 128 * j:128 * (j + 1)],
                                           s_pad[:, 128 * c:128 * (c + 1)], s_id[:])
                    sm["setT"].inc(inst)

            @blk.vector
            def _(v):
                for c4 in range(8):
                    v.wait_ge(sm["setT"].h, c4 + 1)
                    for j in range(4):
                        c = 4 * c4 + j
                        v.tensor_copy(guT[:, 72 * c + 8:72 * c + 72],
                                      p_t[c4 % 2][:, 128 * j:128 * j + 64])
                    sm["setTe"].inc(v.tensor_copy(
                        guT[:].rearrange("p (c e) -> p c e", e=72)[
                            :, 4 * c4:4 * c4 + 4, 0:8],
                        p_t[c4 % 2][:].rearrange("p (c e) -> p c e", e=128)[
                            :, :, 64:72]))

        with nc.Block() as blk:
            @blk.vector
            def _(v):
                sm["setTe"].inc(v.memset(s_pad[:, 0:256], 0.0))

            @blk.gpsimd
            def _(g):
                for c2 in range(16):
                    g.wait_ge(sm["setTe"].h, 1)
                    if c2 >= 1:
                        g.wait_ge(sm["setT"].h, 8 + c2)
                    sm["setld"].inc(g.dma_start(
                        out=s_pad[:].rearrange("p (a e) -> p a e", e=128)[:, 0:2, 0:64],
                        in_=b_dram.rearrange("(a p) e -> p a e", p=128)[
                            :, 2 * c2:2 * c2 + 2, :]), 16)

            @blk.tensor
            def _(t):
                for c2 in range(16):
                    t.wait_ge(sm["setld"].h, sm["setld"].n - 16 * (15 - c2))
                    if c2 >= 1:
                        t.wait_ge(sm["setTe"].h, 1 + c2)
                    for j in range(2):
                        inst = t.transpose(p_t[c2 % 2][:, 128 * j:128 * (j + 1)],
                                           s_pad[:, 128 * j:128 * (j + 1)], s_id[:])
                    sm["setT"].inc(inst)

            @blk.vector
            def _(v):
                for c2 in range(16):
                    v.wait_ge(sm["setT"].h, 8 + c2 + 1)
                    for j in range(2):
                        inst = v.tensor_copy(
                            bTbuf[0:64, 128 * (2 * c2 + j):128 * (2 * c2 + j + 1)],
                            p_t[c2 % 2][0:64, 128 * j:128 * (j + 1)])
                    sm["setTe"].inc(inst)

        with nc.Block() as blk:
            @blk.tensor
            def _(t):
                t.wait_ge(sm["xld"].h, 16)
                for tt in range(NT):
                    if tt >= 1:
                        t.wait_ge(sm["gev"].h, 2 * tt)
                    for c in range(32):
                        inst = t.matmul(
                            p_g[:],
                            xnbuf[:, 1024 * c + 128 * tt:1024 * c + 128 * (tt + 1)],
                            guT[:, 72 * c:72 * (c + 1)],
                            start=(c == 0), stop=(c == 31))
                    sm["gmm"].inc(inst)
                    t.wait_ge(sm["umsk"].h, tt + 1)
                    for r in range(4):
                        m = 4 * tt + r
                        if m >= 2:
                            t.wait_ge(sm["umTe"].h, m - 1)
                        sm["umTp"].inc(t.transpose(p_t[m % 2][0:64, 0:128],
                                                   umbuf[:, 64 * r:64 * (r + 1)],
                                                   s_id[:]))

            @blk.vector
            def _(v):
                vt = sm["vt"]
                for tt in range(NT):
                    v.wait_ge(sm["gmm"].h, tt + 1)
                    v.wait_ge(sm["setld"].h, sm["setld"].n)
                    tick(v, vt, v.tensor_add(s_g[:], p_g[:, 0:8], s_gb[:]))
                    tick(v, vt, v.reduce_max(s_mx[:], s_g[:], axis=AX))
                    tick(v, vt, v.tensor_scalar(s_e[:], s_g[:, 0:4], s_mx[:], None,
                                                ALU.is_equal))
                    tick(v, vt, v.tensor_copy(s_msk[:, 0:1], s_e[:, 0:1]))
                    tick(v, vt, v.tensor_scalar(s_not[:], s_e[:, 0:1], -1.0, 1.0,
                                                ALU.mult, ALU.add))
                    tick(v, vt, v.tensor_mul(s_msk[:, 1:2], s_e[:, 1:2], s_not[:]))
                    tick(v, vt, v.scalar_tensor_tensor(
                        s_not[:], s_msk[:, 1:2], -1.0, s_not[:], ALU.mult, ALU.add))
                    tick(v, vt, v.tensor_mul(s_msk[:, 2:3], s_e[:, 2:3], s_not[:]))
                    tick(v, vt, v.scalar_tensor_tensor(
                        s_not[:], s_msk[:, 2:3], -1.0, s_not[:], ALU.mult, ALU.add))
                    tick(v, vt, v.tensor_mul(s_msk[:, 3:4], s_e[:, 3:4], s_not[:]))
                    tick(v, vt, v.tensor_copy(s_u[:], p_g[:, 8:72]))
                    v.engine_nop().then_inc(sm["gev"].h, 2)
                    sm["gev"].n += 2
                    if tt >= 1:
                        v.wait_ge(sm["umTp"].h, 4 * tt)
                    tick(v, vt, v.memset(umbuf[:], 0.0))
                    for r, R in enumerate(RANKS):
                        tick(v, vt, v.tensor_scalar(
                            umbuf[:, 64 * r:64 * r + R], s_u[:, 0:R],
                            s_msk[:, r:r + 1], None, ALU.mult))
                    v.engine_nop().then_inc(sm["umsk"].h, 1)
                    sm["umsk"].n += 1
                    for r in range(4):
                        m = 4 * tt + r
                        v.wait_ge(sm["umTp"].h, m + 1)
                        sm["umTe"].inc(v.tensor_copy(
                            umTall[0:64, 512 * tt + 128 * r:512 * tt + 128 * (r + 1)],
                            p_t[m % 2][0:64, 0:128]))

        with nc.Block() as blk:
            @blk.gpsimd
            def _(g):
                for og in range(8):
                    for a in range(4):
                        i = og * 4 + a
                        if i >= 2:
                            g.wait_ge(sm["wT"].h, 8 * (i - 1))
                        src = w_dram[512 * og + 128 * a:512 * og + 128 * (a + 1), :]
                        wld2[i % 2].inc(g.dma_start(out=s_w[i % 2][:], in_=src), 16)

            @blk.tensor
            def _(t):
                for og in range(8):
                    for a in range(4):
                        i = og * 4 + a
                        t.wait_ge(wld2[i % 2].h, 16 * (i // 2 + 1))
                        for c4 in range(8):
                            gi = i * 8 + c4
                            if gi >= 2:
                                t.wait_ge(sm["wTe"].h, gi - 1)
                            for j in range(4):
                                c = 4 * c4 + j
                                inst = t.transpose(p_t[gi % 2][:, 128 * j:128 * (j + 1)],
                                                   s_w[i % 2][:, 128 * c:128 * (c + 1)],
                                                   s_id[:])
                            sm["wT"].inc(inst)
                    t.wait_ge(sm["wTe"].h, (og + 1) * 32)
                    for tt in range(NT):
                        k = og * NT + tt
                        if k >= 2:
                            t.wait_ge(sm["fev"].h, k - 1)
                        for c in range(32):
                            t.matmul(p_f[k % 2][:],
                                     xnbuf[:, 1024 * c + 128 * tt:1024 * c + 128 * (tt + 1)],
                                     wTbuf[:, 512 * c:512 * (c + 1)],
                                     start=(c == 0), stop=False)
                        for r in range(4):
                            inst = t.matmul(p_f[k % 2][:],
                                            umTall[0:64, 512 * tt + 128 * r:
                                                   512 * tt + 128 * (r + 1)],
                                            bTbuf[0:64, 512 * og:512 * (og + 1)],
                                            start=False, stop=(r == 3))
                        sm["fmm"].inc(inst)

            @blk.vector
            def _(v):
                for og in range(8):
                    for a in range(4):
                        i = og * 4 + a
                        for c4 in range(8):
                            gi = i * 8 + c4
                            v.wait_ge(sm["wT"].h, gi + 1)
                            dst = wTbuf[:].rearrange("p (c o) -> p c o", o=512)[
                                :, 4 * c4:4 * c4 + 4, 128 * a:128 * (a + 1)]
                            sm["wTe"].inc(v.tensor_copy(
                                dst, p_t[gi % 2][:].rearrange("p (c o) -> p c o", o=128)))
                    for tt in range(NT):
                        k = og * NT + tt
                        v.wait_ge(sm["fmm"].h, k + 1)
                        v.wait_ge(yld2[k % 2].h, 16 * (k // 2 + 1))
                        if k >= 2:
                            v.wait_ge(yst2[k % 2].h, 16 * (k // 2))
                        sm["fev"].inc(v.tensor_add(s_yst[k % 2][:], p_f[k % 2][:],
                                                   s_yt[k % 2][:]))

            @blk.sync
            def _(sync):
                for og in range(8):
                    for tt in range(NT):
                        k = og * NT + tt
                        if k >= 2:
                            sync.wait_ge(sm["fev"].h, k - 1)
                        yld2[k % 2].inc(sync.dma_start(
                            out=s_yt[k % 2][:],
                            in_=y_in_dram[128 * tt:128 * (tt + 1),
                                          512 * og:512 * (og + 1)]), 16)

            @blk.scalar
            def _(s):
                for og in range(8):
                    for tt in range(NT):
                        k = og * NT + tt
                        s.wait_ge(sm["fev"].h, k + 1)
                        yst2[k % 2].inc(s.dma_start(
                            out=y_out_dram[128 * tt:128 * (tt + 1),
                                           512 * og:512 * (og + 1)],
                            in_=s_yst[k % 2][:]), 16)
                s.wait_ge(yst2[0].h, yst2[0].n)
                s.wait_ge(yst2[1].h, yst2[1].n)


def build_ffn0():
    nc = bass.Bass("TRN2", target_bir_lowering=False,
                   detect_race_conditions=(os.environ.get("SIM_NORACE") != "1"))
    xn = nc.dram_tensor("xn", [D, TPC], BF16, kind="ExternalInput").ap()
    w = nc.dram_tensor("w", [D, D], F32, kind="ExternalInput").ap()
    a = nc.dram_tensor("a", [64, D], F32, kind="ExternalInput").ap()
    b = nc.dram_tensor("b", [D, 64], F32, kind="ExternalInput").ap()
    gw = nc.dram_tensor("gw", [4, D], F32, kind="ExternalInput").ap()
    gb = nc.dram_tensor("gb", [128, 8], F32, kind="ExternalInput").ap()
    y_in = nc.dram_tensor("y_in", [TPC, D], F32, kind="ExternalInput").ap()
    y_out = nc.dram_tensor("y_out", [TPC, D], F32, kind="ExternalOutput").ap()
    emit_ffn0(nc, xn, w, a, b, gw, gb, y_in, y_out)
    return nc



def emit_dequant(nc, jobs):
    T = [float(x) for x in NF4]
    es = ExitStack()
    with es:
        W0 = 4096
        qbf = [es.enter_context(_sbt(nc, f"dq_q{i}", [128, W0], BF16)) for i in range(2)]
        ssc = [es.enter_context(_sbt(nc, f"dq_s{i}", [128, W0 // 64], F32)) for i in range(2)]
        tg = es.enter_context(_sbt(nc, "dq_g", [128, W0], BF16))
        th = es.enter_context(_sbt(nc, "dq_h", [128, W0], BF16))
        tb0 = es.enter_context(_sbt(nc, "dq_b0", [128, W0], BF16))
        pA = es.enter_context(_sbt(nc, "dq_pA", [128, W0], BF16))
        pB = es.enter_context(_sbt(nc, "dq_pB", [128, W0], BF16))
        z = [es.enter_context(_sbt(nc, f"dq_z{i}", [128, W0], BF16)) for i in range(4)]
        wout = es.enter_context(_sbt(nc, "dq_w", [128, W0], BF16))
        u1 = es.enter_context(_sbt(nc, "dq_u1", [128, W0], U8))
        u2 = es.enter_context(_sbt(nc, "dq_u2", [128, W0], U8))
        u3 = es.enter_context(_sbt(nc, "dq_u3", [128, W0], U8))
        s_iot = es.enter_context(_sbt(nc, "dq_iot", [128, 128], F32))
        s_id = es.enter_context(_sbt(nc, "dq_id", [128, 128], BF16))
        stg = [es.enter_context(_sbt(nc, f"dq_st{i}", [128, 512], BF16)) for i in range(2)]
        p_t = [es.enter_context(_pst(nc, f"dq_pt{i}", [128, 512], BF16)) for i in range(2)]
        sm = sems(es, nc, ["ident", "qld0", "qld1", "lut", "tp", "tev", "out0", "out1"])
        qld2 = [sm["qld0"], sm["qld1"]]
        out2 = [sm["out0"], sm["out1"]]

        pieces = []
        for ji, (qd, scd, od) in enumerate(jobs):
            R = qd.tensor.shape[0] if hasattr(qd, "tensor") else None
            R = qd.shape[0]
            I = qd.shape[1]
            offs = []
            o = 0
            while o < I:
                w = min(4096, I - o)
                offs.append((o, w))
                o += w
            for tt in range(R // 128):
                for (o, w) in offs:
                    pieces.append((ji, tt, o, w))

        with nc.Block() as blk:
            emit_identity(blk, s_iot, s_id, sm["ident"])

            @blk.gpsimd
            def _(g):
                for i, (ji, tt, o, w) in enumerate(pieces):
                    qd, scd, od = jobs[ji]
                    if i >= 2:
                        g.wait_ge(sm["lut"].h, 2 * (i - 1))
                    qld2[i % 2].inc(g.dma_start(
                        out=qbf[i % 2][:, 0:w],
                        in_=qd[128 * tt:128 * (tt + 1), o:o + w]), 16)
                    qld2[i % 2].inc(g.dma_start(
                        out=ssc[i % 2][:, 0:w // 64],
                        in_=scd[128 * tt:128 * (tt + 1), o // 64:(o + w) // 64]), 16)

            @blk.vector
            def _(v):
                for i, (ji, tt, o, w) in enumerate(pieces):
                    v.wait_ge(qld2[i % 2].h, 32 * (i // 2 + 1))
                    f = qbf[i % 2][:, 0:w]
                    v.tensor_scalar(u3[:, 0:w], f, 8.0, None, ALU.is_ge)
                    v.tensor_scalar(tg[:, 0:w], f, 8.0, None, ALU.is_ge)
                    v.scalar_tensor_tensor(tg[:, 0:w], tg[:, 0:w], -8.0, f,
                                           ALU.mult, ALU.add)
                    v.tensor_scalar(u2[:, 0:w], tg[:, 0:w], 4.0, None, ALU.is_ge)
                    v.tensor_scalar(th[:, 0:w], tg[:, 0:w], 4.0, None, ALU.is_ge)
                    v.scalar_tensor_tensor(th[:, 0:w], th[:, 0:w], -4.0, tg[:, 0:w],
                                           ALU.mult, ALU.add)
                    v.tensor_scalar(u1[:, 0:w], th[:, 0:w], 2.0, None, ALU.is_ge)
                    v.tensor_scalar(tb0[:, 0:w], th[:, 0:w], 2.0, None, ALU.is_ge)
                    v.scalar_tensor_tensor(tb0[:, 0:w], tb0[:, 0:w], -2.0, th[:, 0:w],
                                           ALU.mult, ALU.add)
                    for j in range(4):
                        v.tensor_scalar(pA[:, 0:w], tb0[:, 0:w],
                                        T[4 * j + 1] - T[4 * j], T[4 * j],
                                        ALU.mult, ALU.add)
                        v.tensor_scalar(pB[:, 0:w], tb0[:, 0:w],
                                        T[4 * j + 3] - T[4 * j + 2], T[4 * j + 2],
                                        ALU.mult, ALU.add)
                        v.select(z[j][:, 0:w], u1[:, 0:w], pB[:, 0:w], pA[:, 0:w])
                    v.select(z[0][:, 0:w], u2[:, 0:w], z[1][:, 0:w], z[0][:, 0:w])
                    v.select(z[2][:, 0:w], u2[:, 0:w], z[3][:, 0:w], z[2][:, 0:w])
                    v.select(z[0][:, 0:w], u3[:, 0:w], z[2][:, 0:w], z[0][:, 0:w])
                    if i >= 1:
                        v.wait_ge(sm["tp"].h, sum(
                            pieces[k][3] // 128 for k in range(i)))
                    inst = v.tensor_tensor(
                        wout[:, 0:w].rearrange("p (b j) -> p b j", j=64),
                        z[0][:, 0:w].rearrange("p (b j) -> p b j", j=64),
                        ssc[i % 2][:, 0:w // 64].rearrange(
                            "p (b j) -> p b j", j=1).broadcast_to([128, w // 64, 64]),
                        ALU.mult)
                    sm["lut"].inc(inst, 2)

            @blk.tensor
            def _(t):
                t.wait_ge(sm["ident"].h, 2)
                gcnt = 0
                for i, (ji, tt, o, w) in enumerate(pieces):
                    t.wait_ge(sm["lut"].h, 2 * (i + 1))
                    nch = w // 128
                    for c4 in range((nch + 3) // 4):
                        n4 = min(4, nch - 4 * c4)
                        if gcnt >= 2:
                            t.wait_ge(sm["tev"].h, gcnt - 1)
                        for j in range(n4):
                            c = 4 * c4 + j
                            inst = t.transpose(p_t[gcnt % 2][:, 128 * j:128 * (j + 1)],
                                               wout[:, 128 * c:128 * (c + 1)], s_id[:])
                        sm["tp"].inc(inst, n4)
                        gcnt += 1

            @blk.vector
            def _(v):
                pass

            @blk.scalar
            def _(s):
                gcnt = 0
                tpcnt = 0
                for i, (ji, tt, o, w) in enumerate(pieces):
                    qd, scd, od = jobs[ji]
                    nch = w // 128
                    for c4 in range((nch + 3) // 4):
                        n4 = min(4, nch - 4 * c4)
                        tpcnt += n4
                        s.wait_ge(sm["tp"].h, tpcnt)
                        if gcnt >= 2:
                            s.wait_ge(out2[gcnt % 2].h, 16 * (gcnt // 2))
                        sm["tev"].inc(s.activation(
                            stg[gcnt % 2][:, 0:128 * n4], p_t[gcnt % 2][:, 0:128 * n4],
                            AF.Identity))
                        s.wait_ge(sm["tev"].h, sm["tev"].n)
                        dst = od.rearrange("(a p) t -> p a t", p=128)[
                            :, (o // 128) + 4 * c4:(o // 128) + 4 * c4 + n4,
                            128 * tt:128 * (tt + 1)]
                        src = stg[gcnt % 2][:, 0:128 * n4].rearrange("p (a t) -> p a t", t=128)
                        out2[gcnt % 2].inc(s.dma_start(out=dst, in_=src), 16)
                        gcnt += 1
                s.wait_ge(out2[0].h, out2[0].n)
                s.wait_ge(out2[1].h, out2[1].n)


def build_dequant(nt12=11, nt3=4):
    nc = bass.Bass("TRN2", target_bir_lowering=False,
                   detect_race_conditions=(os.environ.get("SIM_NORACE") != "1"))
    H = 11008
    q1 = nc.dram_tensor("q1", [128 * nt12, D], I32, kind="ExternalInput").ap()
    s1 = nc.dram_tensor("s1", [128 * nt12, D // 64], F32, kind="ExternalInput").ap()
    q2 = nc.dram_tensor("q2", [128 * nt12, D], I32, kind="ExternalInput").ap()
    s2 = nc.dram_tensor("s2", [128 * nt12, D // 64], F32, kind="ExternalInput").ap()
    q3 = nc.dram_tensor("q3", [128 * nt3, H], I32, kind="ExternalInput").ap()
    s3 = nc.dram_tensor("s3", [128 * nt3, H // 64], F32, kind="ExternalInput").ap()
    w1t = nc.dram_tensor("w1t", [D, 128 * nt12], BF16, kind="ExternalOutput").ap()
    w2t = nc.dram_tensor("w2t", [D, 128 * nt12], BF16, kind="ExternalOutput").ap()
    w3t = nc.dram_tensor("w3t", [H, 128 * nt3], BF16, kind="ExternalOutput").ap()
    emit_dequant(nc, [(q1, s1, w1t), (q2, s2, w2t), (q3, s3, w3t)])
    return nc



def emit_ffn1(nc, xn_dram, w1t_dram, w2t_dram, w3t_dram, y_in_dram, y_out_dram,
              w1_src=None, w2_src=None, w3_src=None):
    H = 11008
    NHB = H // 128
    if w1_src is None:
        w1_src = lambda hb: w1t_dram.rearrange("(c p) n -> p c n", p=128)[
            :, :, 128 * hb:128 * (hb + 1)]
    if w2_src is None:
        w2_src = lambda hb: w2t_dram.rearrange("(c p) n -> p c n", p=128)[
            :, :, 128 * hb:128 * (hb + 1)]
    if w3_src is None:
        w3_src = lambda og, hb: w3t_dram[128 * hb:128 * (hb + 1),
                                         512 * og:512 * (og + 1)]
    es = ExitStack()
    with es:
        xnbuf = es.enter_context(_sbt(nc, "xnbuf", [128, 32 * 1024], BF16))
        hbuf = es.enter_context(_sbt(nc, "hbuf", [128, NHB * 512], BF16))
        w1s = [es.enter_context(_sbt(nc, f"w1s{i}", [128, 32 * 128], BF16))
               for i in range(2)]
        w2s = [es.enter_context(_sbt(nc, f"w2s{i}", [128, 32 * 128], BF16))
               for i in range(2)]
        w3s = [es.enter_context(_sbt(nc, f"w3s{i}", [128, 512], BF16))
               for i in range(2)]
        s_sil = es.enter_context(_sbt(nc, "s_sil", [128, 512], BF16))
        s_yt = [es.enter_context(_sbt(nc, f"s_yt{i}", [128, 512], F32))
                for i in range(2)]
        s_yst = [es.enter_context(_sbt(nc, f"s_yst{i}", [128, 512], F32))
                 for i in range(2)]
        p_1 = es.enter_context(_pst(nc, "p_1", [128, 512], F32))
        p_2 = es.enter_context(_pst(nc, "p_2", [128, 512], F32))
        p_o = [es.enter_context(_pst(nc, f"p_o{i}", [128, 512], F32))
               for i in range(4)]
        sm = sems(es, nc, ["xld", "w1l0", "w1l1", "w2l0", "w2l1", "w3l0", "w3l1",
                           "mm1", "mm2", "sil", "hev", "omm", "yld0", "yld1",
                           "oev", "yst0", "yst1"])
        w1l = [sm["w1l0"], sm["w1l1"]]
        w2l = [sm["w2l0"], sm["w2l1"]]
        w3l = [sm["w3l0"], sm["w3l1"]]
        yld2 = [sm["yld0"], sm["yld1"]]
        yst2 = [sm["yst0"], sm["yst1"]]

        with nc.Block() as blk:
            @blk.sync
            def _(sync):
                sm["xld"].inc(sync.dma_start(
                    out=xnbuf[:].rearrange("p (c t) -> p c t", t=1024),
                    in_=xn_dram.rearrange("(c p) t -> p c t", p=128)), 16)

        for tg in range(2):
            with nc.Block() as blk:
                @blk.sync
                def _(sync):
                    for hb in range(NHB):
                        i = tg * NHB + hb
                        if i >= 2:
                            sync.wait_ge(sm["mm1"].h, i - 1)
                        w1l[hb % 2].inc(sync.dma_start(
                            out=w1s[hb % 2][:].rearrange("p (c m) -> p c m", m=128),
                            in_=w1_src(hb)), 16)

                @blk.gpsimd
                def _(g):
                    for hb in range(NHB):
                        i = tg * NHB + hb
                        if i >= 2:
                            g.wait_ge(sm["mm2"].h, i - 1)
                        w2l[hb % 2].inc(g.dma_start(
                            out=w2s[hb % 2][:].rearrange("p (c m) -> p c m", m=128),
                            in_=w2_src(hb)), 16)

                @blk.tensor
                def _(t):
                    t.wait_ge(sm["xld"].h, 16)
                    for hb in range(NHB):
                        i = tg * NHB + hb
                        t.wait_ge(w1l[hb % 2].h, 16 * (tg * NHB // 2 + hb // 2 + 1))
                        t.wait_ge(w2l[hb % 2].h, 16 * (tg * NHB // 2 + hb // 2 + 1))
                        if i >= 1:
                            t.wait_ge(sm["sil"].h, i)
                            t.wait_ge(sm["hev"].h, i)
                        for c in range(32):
                            inst = t.matmul(
                                p_1[:],
                                w1s[hb % 2][:, 128 * c:128 * (c + 1)],
                                xnbuf[:, 1024 * c + 512 * tg:1024 * c + 512 * (tg + 1)],
                                start=(c == 0), stop=(c == 31))
                        sm["mm1"].inc(inst)
                        for c in range(32):
                            inst = t.matmul(
                                p_2[:],
                                w2s[hb % 2][:, 128 * c:128 * (c + 1)],
                                xnbuf[:, 1024 * c + 512 * tg:1024 * c + 512 * (tg + 1)],
                                start=(c == 0), stop=(c == 31))
                        sm["mm2"].inc(inst)

                @blk.scalar
                def _(s):
                    for hb in range(NHB):
                        i = tg * NHB + hb
                        s.wait_ge(sm["mm1"].h, i + 1)
                        if i >= 1:
                            s.wait_ge(sm["hev"].h, i)
                        sm["sil"].inc(s.activation(s_sil[:], p_1[:], AF.Silu))

                @blk.vector
                def _(v):
                    for hb in range(NHB):
                        i = tg * NHB + hb
                        v.wait_ge(sm["sil"].h, i + 1)
                        v.wait_ge(sm["mm2"].h, i + 1)
                        sm["hev"].inc(v.tensor_mul(
                            hbuf[:, 512 * hb:512 * (hb + 1)], s_sil[:], p_2[:]))

            with nc.Block() as blk:
                @blk.sync
                def _(sync):
                    for og in range(8):
                        for hb in range(NHB):
                            i = (tg * 8 + og) * NHB + hb
                            if i >= 2:
                                sync.wait_ge(sm["omm"].h, 4 * (i - 1))
                            w3l[hb % 2].inc(sync.dma_start(
                                out=w3s[hb % 2][:], in_=w3_src(og, hb)), 16)

                @blk.tensor
                def _(t):
                    for og in range(8):
                        t.wait_ge(sm["oev"].h, (tg * 8 + og) * 4)
                        for hb in range(NHB):
                            i = (tg * 8 + og) * NHB + hb
                            t.wait_ge(w3l[hb % 2].h,
                                      16 * ((tg * 8 + og) * NHB // 2 + hb // 2 + 1))
                            for ts in range(4):
                                inst = t.matmul(
                                    p_o[ts][:],
                                    hbuf[:, 512 * hb + 128 * ts:512 * hb + 128 * (ts + 1)],
                                    w3s[hb % 2][:],
                                    start=(hb == 0), stop=(hb == NHB - 1))
                            sm["omm"].inc(inst, 4)

                @blk.gpsimd
                def _(g):
                    for og in range(8):
                        for ts in range(4):
                            i = (tg * 8 + og) * 4 + ts
                            if i >= 2:
                                g.wait_ge(sm["oev"].h, i - 1)
                            yld2[i % 2].inc(g.dma_start(
                                out=s_yt[i % 2][:],
                                in_=y_in_dram[512 * tg + 128 * ts:512 * tg + 128 * (ts + 1),
                                              512 * og:512 * (og + 1)]), 16)

                @blk.vector
                def _(v):
                    for og in range(8):
                        v.wait_ge(sm["omm"].h, 4 * (tg * 8 + og + 1) * NHB)
                        for ts in range(4):
                            i = (tg * 8 + og) * 4 + ts
                            v.wait_ge(yld2[i % 2].h, 16 * (i // 2 + 1))
                            if i >= 2:
                                v.wait_ge(yst2[i % 2].h, 16 * (i // 2))
                            sm["oev"].inc(v.tensor_add(s_yst[i % 2][:], p_o[ts][:],
                                                       s_yt[i % 2][:]))

                @blk.scalar
                def _(s):
                    for og in range(8):
                        for ts in range(4):
                            i = (tg * 8 + og) * 4 + ts
                            s.wait_ge(sm["oev"].h, i + 1)
                            yst2[i % 2].inc(s.dma_start(
                                out=y_out_dram[512 * tg + 128 * ts:
                                               512 * tg + 128 * (ts + 1),
                                               512 * og:512 * (og + 1)],
                                in_=s_yst[i % 2][:]), 16)
        with nc.Block() as blk:
            @blk.scalar
            def _(s):
                s.wait_ge(yst2[0].h, yst2[0].n)
                s.wait_ge(yst2[1].h, yst2[1].n)


def build_ffn1():
    nc = bass.Bass("TRN2", target_bir_lowering=False,
                   detect_race_conditions=(os.environ.get("SIM_NORACE") != "1"))
    H = 11008
    xn = nc.dram_tensor("xn", [D, TPC], BF16, kind="ExternalInput").ap()
    w1t = nc.dram_tensor("w1t", [D, H], BF16, kind="ExternalInput").ap()
    w2t = nc.dram_tensor("w2t", [D, H], BF16, kind="ExternalInput").ap()
    w3t = nc.dram_tensor("w3t", [H, D], BF16, kind="ExternalInput").ap()
    y_in = nc.dram_tensor("y_in", [TPC, D], F32, kind="ExternalInput").ap()
    y_out = nc.dram_tensor("y_out", [TPC, D], F32, kind="ExternalOutput").ap()
    emit_ffn1(nc, xn, w1t, w2t, w3t, y_in, y_out)
    return nc



def build_ln_qkv0():
    nc = bass.Bass("TRN2", target_bir_lowering=False)
    x = nc.dram_tensor("x", [TPC, D], F32, kind="ExternalInput").ap()
    wqkv = nc.dram_tensor("wqkv", [D, 3072], F32, kind="ExternalInput").ap()
    bq_pt = nc.dram_tensor("bq_pt", [128, 8], F32, kind="ExternalInput").ap()
    bk_pt = nc.dram_tensor("bk_pt", [128, 8], F32, kind="ExternalInput").ap()
    bv_bc = nc.dram_tensor("bv_bc", [128, 1024], F32, kind="ExternalInput").ap()
    xn_fm = nc.dram_tensor("xn_fm", [D, TPC], BF16, kind="ExternalOutput").ap()
    q_fm = nc.dram_tensor("q_fm", [1024, TPC], BF16, kind="ExternalOutput").ap()
    k_fm = nc.dram_tensor("k_fm", [1024, TPC], BF16, kind="ExternalOutput").ap()
    v_tm = nc.dram_tensor("v_tm", [TPC, 1024], BF16, kind="ExternalOutput").ap()
    projs = [
        dict(w=wqkv, col0=0, dout=1024, bias=bq_pt, sbias0=0, out=q_fm, kind="fm"),
        dict(w=wqkv, col0=1024, dout=1024, bias=bk_pt, sbias0=8, out=k_fm, kind="fm"),
        dict(w=wqkv, col0=2048, dout=1024, bias=bv_bc, sbias0=0, out=v_tm, kind="tm"),
    ]
    emit_ln_proj(nc, x, xn_fm, projs)
    return nc


def build_ln_qkv1():
    nc = bass.Bass("TRN2", target_bir_lowering=False)
    x = nc.dram_tensor("x", [TPC, D], F32, kind="ExternalInput").ap()
    wq = nc.dram_tensor("wq", [D, D], F32, kind="ExternalInput").ap()
    wkv = nc.dram_tensor("wkv", [D, 256], F32, kind="ExternalInput").ap()
    xn_fm = nc.dram_tensor("xn_fm", [D, TPC], BF16, kind="ExternalOutput").ap()
    q_fm = nc.dram_tensor("q_fm", [D, TPC], BF16, kind="ExternalOutput").ap()
    k_fm = nc.dram_tensor("k_fm", [128, TPC], BF16, kind="ExternalOutput").ap()
    v_tm = nc.dram_tensor("v_tm", [TPC, 128], BF16, kind="ExternalOutput").ap()
    projs = [
        dict(w=wq, col0=0, dout=4096, bias=None, sbias0=0, out=q_fm, kind="fm"),
        dict(w=wkv, col0=0, dout=128, bias=None, sbias0=0, out=k_fm, kind="fm"),
        dict(w=wkv, col0=128, dout=128, bias=None, sbias0=0, out=v_tm, kind="tm"),
    ]
    emit_ln_proj(nc, x, xn_fm, projs)
    return nc



def own_token_rows(core):
    b, j = divmod(core, 4)
    tiles = [j + 4 * t for t in range(NT)]
    return np.concatenate([np.arange(128 * g, 128 * (g + 1)) for g in tiles])


def _bc(v, n):
    return np.ascontiguousarray(np.broadcast_to(np.asarray(v, np.float32), (128, n)))


def kernel(L0_attn_x, L0_attn_Wqkv, L0_attn_bqkv, L0_attn_Wo, L0_attn_bo,
           L0_ffn_W, L0_ffn_b, L0_ffn_A, L0_ffn_B, L0_ffn_gate_w, L0_ffn_gate_b,
           L1_attn_Wq, L1_attn_Wkv, L1_attn_Wo, L1_ffn_w1_q, L1_ffn_w1_scale,
           L1_ffn_w2_q, L1_ffn_w2_scale, L1_ffn_w3_q, L1_ffn_w3_scale):
    H = 11008
    x = np.ascontiguousarray(np.asarray(L0_attn_x, np.float32))
    wqkv = np.ascontiguousarray(np.asarray(L0_attn_Wqkv, np.float32))
    bqkv = np.asarray(L0_attn_bqkv, np.float32)
    wo0 = np.ascontiguousarray(np.asarray(L0_attn_Wo, np.float32))
    bo0 = np.asarray(L0_attn_bo, np.float32)
    w0 = np.ascontiguousarray(np.asarray(L0_ffn_W, np.float32))
    b0f = np.asarray(L0_ffn_b, np.float32)
    a0 = np.ascontiguousarray(np.asarray(L0_ffn_A, np.float32))
    b0m = np.ascontiguousarray(np.asarray(L0_ffn_B, np.float32))
    gw0 = np.ascontiguousarray(np.asarray(L0_ffn_gate_w, np.float32))
    gb0 = np.asarray(L0_ffn_gate_b, np.float32)
    wq1 = np.ascontiguousarray(np.asarray(L1_attn_Wq, np.float32))
    wkv1 = np.ascontiguousarray(np.asarray(L1_attn_Wkv, np.float32))
    wo1 = np.ascontiguousarray(np.asarray(L1_attn_Wo, np.float32))

    rows = [own_token_rows(c) for c in range(NCORE)]
    x_own = [np.ascontiguousarray(x[c // 4][rows[c]]) for c in range(NCORE)]

    bq_pt = np.ascontiguousarray(bqkv[0:1024].reshape(8, 128).T)
    bk_pt = np.ascontiguousarray(bqkv[1024:2048].reshape(8, 128).T)
    bv_bc = _bc(bqkv[2048:3072], 1024)
    bo_bc = _bc(bo0, D)
    bffn_bc = _bc(b0f, D)
    gb_bc = np.full((128, 8), -1e30, np.float32)
    gb_bc[:, 0:4] = gb0
    gb_bc = np.ascontiguousarray(gb_bc)

    def assemble_seq(parts, axis):
        full = [None, None]
        for b in range(B):
            gshape = list(parts[4 * b].shape)
            gshape[axis] = S
            g = np.empty(gshape, parts[0].dtype)
            for j in range(4):
                p = parts[4 * b + j]
                for t in range(NT):
                    gsl = [slice(None)] * p.ndim
                    lsl = [slice(None)] * p.ndim
                    gsl[axis] = slice(128 * (4 * t + j), 128 * (4 * t + j + 1))
                    lsl[axis] = slice(128 * t, 128 * (t + 1))
                    g[tuple(gsl)] = p[tuple(lsl)]
            full[b] = g
        return full

    nc = build_ln_qkv0()
    maps = [dict(x=x_own[c], wqkv=wqkv, bq_pt=bq_pt, bk_pt=bk_pt, bv_bc=bv_bc)
            for c in range(NCORE)]
    r1 = run_bass_kernel_spmd(nc, maps, core_ids=list(range(NCORE))).results
    xn0 = [r["xn_fm"] for r in r1]
    k0_full = assemble_seq([r["k_fm"] for r in r1], axis=1)
    v0_full = assemble_seq([r["v_tm"] for r in r1], axis=0)

    nc = build_attn0()
    maps = [dict(q=r1[c]["q_fm"], k=np.ascontiguousarray(k0_full[c // 4]),
                 v=np.ascontiguousarray(v0_full[c // 4]), wo=wo0, x=x_own[c],
                 bias1=bo_bc, bias2=bffn_bc) for c in range(NCORE)]
    r2 = run_bass_kernel_spmd(nc, maps, core_ids=list(range(NCORE))).results

    nc = build_ffn0()
    maps = [dict(xn=xn0[c], w=w0, a=a0, b=b0m, gw=gw0, gb=gb_bc,
                 y_in=r2[c]["y"]) for c in range(NCORE)]
    r3 = run_bass_kernel_spmd(nc, maps, core_ids=list(range(NCORE))).results

    nc = build_ln_qkv1()
    maps = [dict(x=r3[c]["y_out"], wq=wq1, wkv=wkv1) for c in range(NCORE)]
    r4 = run_bass_kernel_spmd(nc, maps, core_ids=list(range(NCORE))).results
    xn1 = [r["xn_fm"] for r in r4]
    k1_full = assemble_seq([r["k_fm"] for r in r4], axis=1)
    v1_full = assemble_seq([r["v_tm"] for r in r4], axis=0)

    i_idx = np.arange(128)[:, None]
    jj_idx = np.arange(512)[None, :]
    masks = []
    for c in range(NCORE):
        j = c % 4
        md = np.where(jj_idx - i_idx - 128 * j <= 0, -128.0 * j, -1e30)
        masks.append(np.ascontiguousarray(np.concatenate(
            [md, np.full((128, 512), -128.0 * j)], axis=1).astype(np.float32)))
    nc = build_attn1_core()
    maps = [dict(q=r4[c]["q_fm"], k=np.ascontiguousarray(k1_full[c // 4]),
                 v=np.ascontiguousarray(v1_full[c // 4]), msk=masks[c])
            for c in range(NCORE)]
    r5 = run_bass_kernel_spmd(nc, maps, core_ids=list(range(NCORE))).results

    nc = build_proj_res()
    maps = [dict(act=r5[c]["attnfm"], w=wo1, y_in=r3[c]["y_out"])
            for c in range(NCORE)]
    r6 = run_bass_kernel_spmd(nc, maps, core_ids=list(range(NCORE))).results

    def shard12(qm, sc, c):
        idx = np.minimum(np.arange(11 * c, 11 * c + 11), 85)
        qs = np.concatenate([qm[128 * g:128 * (g + 1)] for g in idx])
        ss = np.concatenate([sc[128 * g:128 * (g + 1)] for g in idx])
        return np.ascontiguousarray(qs), np.ascontiguousarray(ss)

    q1m = np.asarray(L1_ffn_w1_q, np.int32)
    s1m = np.asarray(L1_ffn_w1_scale, np.float32)
    q2m = np.asarray(L1_ffn_w2_q, np.int32)
    s2m = np.asarray(L1_ffn_w2_scale, np.float32)
    q3m = np.asarray(L1_ffn_w3_q, np.int32)
    s3m = np.asarray(L1_ffn_w3_scale, np.float32)
    nc = build_dequant()
    maps = []
    for c in range(NCORE):
        qa, sa = shard12(q1m, s1m, c)
        qb, sb = shard12(q2m, s2m, c)
        qc = np.ascontiguousarray(q3m[512 * c:512 * (c + 1)])
        scc = np.ascontiguousarray(s3m[512 * c:512 * (c + 1)])
        maps.append(dict(q1=qa, s1=sa, q2=qb, s2=sb, q3=qc, s3=scc))
    r7 = run_bass_kernel_spmd(nc, maps, core_ids=list(range(NCORE))).results
    w1t = np.empty((D, H), BF)
    w2t = np.empty((D, H), BF)
    for c in range(NCORE):
        for k in range(11):
            g = 11 * c + k
            if g <= 85:
                w1t[:, 128 * g:128 * (g + 1)] = r7[c]["w1t"][:, 128 * k:128 * (k + 1)]
                w2t[:, 128 * g:128 * (g + 1)] = r7[c]["w2t"][:, 128 * k:128 * (k + 1)]
    w3t = np.concatenate([r7[c]["w3t"] for c in range(NCORE)], axis=1)
    w1t = np.ascontiguousarray(w1t)
    w2t = np.ascontiguousarray(w2t)
    w3t = np.ascontiguousarray(w3t)

    nc = build_ffn1()
    maps = [dict(xn=xn1[c], w1t=w1t, w2t=w2t, w3t=w3t, y_in=r6[c]["y_out"])
            for c in range(NCORE)]
    r8 = run_bass_kernel_spmd(nc, maps, core_ids=list(range(NCORE))).results

    out = np.empty((B, S, D), np.float32)
    for c in range(NCORE):
        out[c // 4][rows[c]] = r8[c]["y_out"]
    return out
